# version 04
# baseline (speedup 1.0000x reference)
import numpy as np
import ml_dtypes
import concourse.bass as bass
import concourse.mybir as mybir
from concourse.bass_utils import run_bass_kernel_spmd
from contextlib import ExitStack

F32 = mybir.dt.float32
BF16 = mybir.dt.bfloat16
ALU = mybir.AluOpType
ACT = mybir.ActivationFunctionType
AX = mybir.AxisListType

B, S, DM = 4, 4096, 2048
T = 2048
H, KVH, HD = 16, 4, 128
IFF = 4096
NSC, SCT, NCH = 4, 512, 4
C23 = float(1.5 * 2 ** 23)
C52 = float(1.5 * 2 ** 52)
CLIP = 127.499

ENGINES = ("sync", "gpsimd", "vector", "scalar", "tensor")
INC = {"sync": 16, "gpsimd": 16, "vector": 1, "scalar": 1, "tensor": 1}


def merge(*toks):
    out = {}
    for t in toks:
        if not t:
            continue
        for k, v in t.items():
            if out.get(k, -1) < v:
                out[k] = v
    return out


DEBUG = False


def build_program():
    nc = bass.Bass("TRN2", target_bir_lowering=False)

    def din(name, shape, dt=F32):
        return nc.dram_tensor(name, shape, dt, kind="ExternalInput").ap()

    x_own = din("x_own", [T, DM])
    x_pre = din("x_pre", [T, DM])
    pflag = din("pflag", [128, 1])
    wqT = din("wqT", [DM, DM])
    wkT = din("wkT", [DM, 512])
    wvT = din("wvT", [DM, 512])
    woT = din("woT", [DM, DM])
    wupT = din("wupT", [DM, IFF])
    wdnT = din("wdnT", [IFF, DM])
    cos_own = din("cos_own", [128, T])
    sin_own = din("sin_own", [128, T])
    cos_pre = din("cos_pre", [128, T])
    sin_pre = din("sin_pre", [128, T])
    maskT = din("maskT", [128, 128])
    identf = din("identf", [128, 128])
    y_out = nc.dram_tensor("y", [T, DM], F32, kind="ExternalOutput").ap()
    s_out = nc.dram_tensor("s_out", [DM, T], BF16, kind="Internal").ap()
    s_x1 = nc.dram_tensor("s_x1", [T, DM], F32, kind="Internal").ap()
    s_ru = nc.dram_tensor("s_ru", [T, IFF], F32, kind="Internal").ap()
    s_wdn = nc.dram_tensor("s_wdn", [IFF, DM], BF16, kind="Internal").ap()
    if DEBUG:
        dbg_out = nc.dram_tensor("dbg_out", [DM, T], BF16, kind="ExternalOutput").ap()
        dbg_x1 = nc.dram_tensor("dbg_x1", [T, DM], F32, kind="ExternalOutput").ap()
        dbg_kf = nc.dram_tensor("dbg_kf", [128, 2048], BF16, kind="ExternalOutput").ap()
        dbg_vt = nc.dram_tensor("dbg_vt", [128, 2048], BF16, kind="ExternalOutput").ap()
        dbg_qf = nc.dram_tensor("dbg_qf", [128, 8192], BF16, kind="ExternalOutput").ap()
        dbg_xnf = nc.dram_tensor("dbg_xnf", [128, 8192], BF16, kind="ExternalOutput").ap()
        dbg_st0 = nc.dram_tensor("dbg_st0", [128, 512], BF16, kind="ExternalOutput").ap()
        dbg_zb0 = nc.dram_tensor("dbg_zb0", [128, 4], BF16, kind="ExternalOutput").ap()
        dbg_ru = nc.dram_tensor("dbg_ru", [512, IFF], F32, kind="ExternalOutput").ap()
        dbg_wdnq = nc.dram_tensor("dbg_wdnq", [IFF, DM], BF16, kind="ExternalOutput").ap()
        dbg_sc = nc.dram_tensor("dbg_sc", [128, 52], F32, kind="ExternalOutput").ap()

    with ExitStack() as st:
        sb = lambda n, s, d=F32: st.enter_context(nc.sbuf_tensor(n, s, d))
        pool1 = sb("pool1", [128, 16 * 2048], BF16)
        poolW = sb("poolW", [128, 16384], BF16)
        qf16 = sb("qf16", [128, 8192], BF16)
        xnf = sb("xnf", [128, 8192], BF16)
        T_a = sb("T_a", [128, 2048])
        T_b = sb("T_b", [128, 2048])
        T_c = sb("T_c", [128, 2048])
        sqt = sb("sqt", [128, 2048], BF16)
        kf = sb("kf", [128, 2048], BF16)
        kft = sb("kft", [128, 2048], BF16)
        vt = sb("vt", [128, 2048], BF16)
        r1 = sb("r1", [128, 512])
        r2 = sb("r2", [128, 512])
        stf = sb("stf", [128, 512])
        stb = sb("stb", [128, 512], BF16)
        abf = sb("abf", [128, 1024], BF16)
        otm = sb("otm", [128, 2048], BF16)
        otf = sb("otf", [128, 512], BF16)
        cosb = sb("cosb", [128, 512])
        sinb = sb("sinb", [128, 512])
        maskb = sb("maskb", [128, 128])
        identb = sb("identb", [128, 128], BF16)
        onesb = sb("onesb", [128, 1], BF16)
        pfl = sb("pfl", [128, 1])
        fdom = sb("fdom", [128, 1])
        zf = sb("zf", [128, 4])
        zb = sb("zb", [128, 4], BF16)
        zr = sb("zr", [128, 4])
        dent = sb("dent", [128, 16])
        rden = sb("rden", [128, 16])
        stats = sb("stats", [128, 32])
        acc = sb("acc", [128, 4])
        tmp1 = sb("tmp1", [128, 2])
        tmp2 = sb("tmp2", [128, 2])
        tmpf = sb("tmpf", [64, 2])
        scl = sb("scl", [128, 2])
        hsc = sb("hsc", [128, 2])
        nhs = sb("nhs", [128, 2])
        cfac = sb("cfac", [128, 16])
        rumac = sb("rumac", [128, 16])
        resc = sb("resc", [128, 16])
        sq2s = sb("sq2s", [128, 16])

        ps0 = st.enter_context(nc.psum_tensor("ps0", [128, 1024], F32))
        ps1 = st.enter_context(nc.psum_tensor("ps1", [128, 1024], F32))
        ps2 = st.enter_context(nc.psum_tensor("ps2", [128, 512], F32))
        psD = st.enter_context(nc.psum_tensor("psD", [128, 16], F32))
        psT = st.enter_context(nc.psum_tensor("psT", [128, 512], BF16))
        SEM = {e: st.enter_context(nc.semaphore(f"sem_{e}")) for e in ENGINES}
        block = st.enter_context(nc.Block())

        EV = {e: [] for e in ENGINES}
        cnt = {e: 0 for e in ENGINES}

        def ev(e, fn, *waits):
            w = merge(*waits)
            cnt[e] += INC[e]
            EV[e].append(("op", w, fn))
            return {e: cnt[e]}

        def vself():
            return {"vector": cnt["vector"]}

        def bar():
            snap = dict(cnt)
            for e in ENGINES:
                w = {k: v for k, v in snap.items() if k != e}
                EV[e].append(("bar", w, None))

        tk_mask = ev("sync", lambda g: g.dma_start(out=maskb[:], in_=maskT))
        tk_id = ev("sync", lambda g: g.dma_start(out=T_c[:, 0:128], in_=identf))
        tk_pfl = ev("sync", lambda g: g.dma_start(out=pfl[:], in_=pflag))
        tk_idb = ev("vector", lambda g: g.tensor_copy(identb[:], T_c[:, 0:128]), tk_id)
        tk_ones = ev("vector", lambda g: g.memset(onesb[:], 1.0))
        t0 = ev("vector", lambda g: g.memset(fdom[:], 1.0))
        t0 = ev("vector", lambda g: g.tensor_scalar(fdom[:], fdom[:], float(2 ** -40), 1.0, ALU.add, ALU.subtract), t0, vself())
        tk_fdom = ev("vector", lambda g: g.tensor_scalar_mul(fdom[:], fdom[:], float(2 ** 40)), t0, vself())
        ev("vector", lambda g: g.memset(acc[:], 0.0))
        ev("vector", lambda g: g.memset(zf[:], 0.0))
        ev("vector", lambda g: g.memset(rumac[:], 0.0))
        ev("vector", lambda g: g.memset(stf[:], 0.0))

        stage_free = [None, None, None, None]

        def wload(si, dst, dc, src_ap, w=512):
            slot = si % 4
            a, b = slot * 512, slot * 512 + w
            tk_d = ev("sync", lambda g: g.dma_start(out=T_c[:, a:b], in_=src_ap), stage_free[slot])
            tk_c = ev("scalar", lambda g: g.activation(dst[:, dc:dc + w], T_c[:, a:b], ACT.Copy), tk_d)
            stage_free[slot] = tk_c
            return tk_c

        si = 0
        tk_wk, tk_wv, tk_wq = {}, {}, {}
        for dt_ in range(16):
            tk_wk[dt_] = wload(si, poolW, dt_ * 512, wkT[dt_ * 128:(dt_ + 1) * 128, :]); si += 1
        for dt_ in range(16):
            tk_wv[dt_] = wload(si, poolW, 8192 + dt_ * 512, wvT[dt_ * 128:(dt_ + 1) * 128, :]); si += 1

        state = {
            "tk_stb": None, "tk_zb": None, "tk_stf": None, "tk_zf": None,
            "tk_ps0": None, "tk_ps1": None, "tk_ps2": None, "tk_psD": None,
            "tk_psT": None,
        }

        def rope_elu(ps, off, dst, dcol, tk_ps, tk_cos, tk_prev_tmp):
            w0 = merge(tk_ps, tk_cos, tk_prev_tmp)
            ta = ev("vector", lambda g: g.tensor_scalar_mul(r1[0:64, :], ps[64:128, off:off + 512], -1.0), w0)
            tb = ev("vector", lambda g: g.tensor_scalar_mul(r1[64:128, :], ps[0:64, off:off + 512], 1.0), w0)
            tc2 = ev("vector", lambda g: g.tensor_tensor(r2[:], ps[:, off:off + 512], cosb[:], ALU.mult), w0)
            td = ev("vector", lambda g: g.tensor_tensor(r1[:], r1[:], sinb[:], ALU.mult), ta, tb, vself())
            te = ev("vector", lambda g: g.tensor_tensor(r2[:], r2[:], r1[:], ALU.add), tc2, td, vself())
            tf = ev("vector", lambda g: g.tensor_scalar_min(r1[:], r2[:], 0.0), te, vself())
            tg = ev("scalar", lambda g: g.activation(r1[:], r1[:], ACT.Exp), tf)
            th = ev("vector", lambda g: g.scalar_tensor_tensor(dst[:, dcol:dcol + 512], r2[:], 0.0, r1[:], ALU.max, ALU.add), tg)
            return th

        def super_chunk(sc, is_prefix):
            bar()
            xsrc = x_pre if is_prefix else x_own
            cs, ss = (cos_pre, sin_pre) if is_prefix else (cos_own, sin_own)
            tkc = ev("sync", lambda g: g.dma_start(out=cosb[:], in_=cs[:, sc * SCT:(sc + 1) * SCT]))
            tks = ev("sync", lambda g: g.dma_start(out=sinb[:], in_=ss[:, sc * SCT:(sc + 1) * SCT]))
            tk_cos = merge(tkc, tks)
            xb = [T_a, T_b, T_a, T_b]
            tk_x = [None] * 4
            tk_sq = None
            tk_xfree = [None, None]
            tk_xnf = []
            for c in range(4):
                tk_x[c] = ev("sync", lambda g, c=c: g.dma_start(
                    out=xb[c][:], in_=xsrc[(sc * NCH + c) * 128:(sc * NCH + c) * 128 + 128, :]), tk_xfree[c % 2])
                sc0 = c * 8
                t1 = ev("vector", lambda g, c=c, sc0=sc0: g.reduce_sum(stats[:, sc0:sc0 + 1], xb[c][:], AX.X), tk_x[c])
                tq = ev("scalar", lambda g, c=c: g.activation(sqt[:], xb[c][:], ACT.Square), tk_x[c], tk_sq)
                t2 = ev("vector", lambda g, sc0=sc0: g.reduce_sum(stats[:, sc0 + 1:sc0 + 2], sqt[:], AX.X), tq)
                tm = ev("vector", lambda g, sc0=sc0: g.tensor_scalar_mul(stats[:, sc0:sc0 + 1], stats[:, sc0:sc0 + 1], 1.0 / DM), t1, vself())
                tq2 = ev("vector", lambda g, sc0=sc0: g.tensor_scalar_mul(stats[:, sc0 + 1:sc0 + 2], stats[:, sc0 + 1:sc0 + 2], 1.0 / DM), t2, vself())
                tv = ev("vector", lambda g, sc0=sc0: g.tensor_tensor(stats[:, sc0 + 2:sc0 + 3], stats[:, sc0:sc0 + 1], stats[:, sc0:sc0 + 1], ALU.mult), tm, vself())
                tv2 = ev("vector", lambda g, sc0=sc0: g.tensor_sub(stats[:, sc0 + 1:sc0 + 2], stats[:, sc0 + 1:sc0 + 2], stats[:, sc0 + 2:sc0 + 3]), tq2, tv, vself())
                tsd = ev("scalar", lambda g, sc0=sc0: g.activation(stats[:, sc0 + 2:sc0 + 3], stats[:, sc0 + 1:sc0 + 2], ACT.Sqrt, bias=np.float32(1e-5)), tv2)
                tr = ev("vector", lambda g, sc0=sc0: g.reciprocal(stats[:, sc0 + 3:sc0 + 4], stats[:, sc0 + 2:sc0 + 3]), tsd)
                tmr = ev("vector", lambda g, sc0=sc0: g.scalar_tensor_tensor(stats[:, sc0 + 4:sc0 + 5], stats[:, sc0:sc0 + 1], -1.0, stats[:, sc0 + 3:sc0 + 4], ALU.mult, ALU.mult), tr, vself())
                txn = ev("scalar", lambda g, c=c, sc0=sc0: g.activation(
                    sqt[:], xb[c][:], ACT.Identity, bias=stats[:, sc0 + 4:sc0 + 5], scale=stats[:, sc0 + 3:sc0 + 4]),
                    merge(tr, tmr), tk_sq, tk_x[c], t2)
                tk_xfree[c % 2] = txn
                for grp in range(4):
                    tts = []
                    for k in range(4):
                        dt_ = grp * 4 + k
                        tts.append(ev("tensor", lambda g, dt_=dt_, k=k: g.transpose(
                            psT[:, k * 128:(k + 1) * 128], sqt[:, dt_ * 128:(dt_ + 1) * 128], identb[:]),
                            txn, tk_idb, state["tk_psT"]))
                    def _evx(g, grp=grp, c=c):
                        for k in range(3):
                            g.tensor_copy(xnf[:, (grp * 4 + k) * 512 + c * 128:(grp * 4 + k) * 512 + c * 128 + 128],
                                          psT[:, k * 128:(k + 1) * 128])
                        return g.tensor_copy(xnf[:, (grp * 4 + 3) * 512 + c * 128:(grp * 4 + 3) * 512 + c * 128 + 128],
                                             psT[:, 384:512])
                    tev = ev("vector", _evx, merge(*tts))
                    state["tk_psT"] = tev
                    tk_xnf.append(tev)
                tk_sq = tk_xnf[-1]
            tk_xnf_all = merge(*tk_xnf)

            pshalf = [(ps0, 0), (ps0, 512), (ps1, 0), (ps1, 512)]
            psfree = {0: state["tk_ps0"], 1: state["tk_ps0"], 2: state["tk_ps1"], 3: state["tk_ps1"]}

            tk_kf = {}
            tk_tmp = None
            for g_ in range(4):
                ps, off = pshalf[g_]
                mm = None
                for dt_ in range(16):
                    mm = ev("tensor", lambda g, ps=ps, off=off, dt_=dt_, g_=g_, s=(dt_ == 0), p=(dt_ == 15): g.matmul(
                        ps[:, off:off + 512], poolW[:, dt_ * 512 + g_ * 128:dt_ * 512 + g_ * 128 + 128],
                        xnf[:, dt_ * 512:(dt_ + 1) * 512], start=s, stop=p),
                        tk_wk[dt_], tk_xnf_all, psfree[g_])
                tk = rope_elu(ps, off, kf, g_ * 512, mm, tk_cos, tk_tmp)
                tk_tmp = tk
                tk_kf[g_] = tk
                psfree[g_] = tk
            tk_vt = {}
            for c in range(4):
                ps, off = pshalf[c]
                mm = None
                for dt_ in range(16):
                    mm = ev("tensor", lambda g, ps=ps, off=off, dt_=dt_, c=c, s=(dt_ == 0), p=(dt_ == 15): g.matmul(
                        ps[:, off:off + 512],
                        xnf[:, dt_ * 512 + c * 128:dt_ * 512 + c * 128 + 128],
                        poolW[:, 8192 + dt_ * 512:8192 + (dt_ + 1) * 512], start=s, stop=p),
                        tk_wv[dt_], tk_xnf_all, psfree[c])
                tk_vt[c] = ev("scalar", lambda g, ps=ps, off=off, c=c: g.activation(
                    vt[:, c * 512:(c + 1) * 512], ps[:, off:off + 512], ACT.Copy), mm)
                psfree[c] = tk_vt[c]
            tk_kft = {}
            for c in range(4):
                for g_ in range(4):
                    tt = ev("tensor", lambda g, c=c, g_=g_: g.transpose(
                        psT[:, g_ * 128:(g_ + 1) * 128],
                        kf[:, g_ * 512 + c * 128:g_ * 512 + c * 128 + 128], identb[:]),
                        tk_kf[g_], state["tk_psT"])
                    tev = ev("vector", lambda g, c=c, g_=g_: g.tensor_copy(
                        kft[:, (c * 4 + g_) * 128:(c * 4 + g_) * 128 + 128],
                        psT[:, g_ * 128:(g_ + 1) * 128]), tt)
                    state["tk_psT"] = tev
                    tk_kft[(c, g_)] = tev
            tk_qf = {}
            if not is_prefix:
                for h in range(16):
                    ps, off = pshalf[h % 4]
                    mm = None
                    for dt_ in range(16):
                        mm = ev("tensor", lambda g, ps=ps, off=off, dt_=dt_, h=h, s=(dt_ == 0), p=(dt_ == 15): g.matmul(
                            ps[:, off:off + 512], pool1[:, dt_ * 2048 + h * 128:dt_ * 2048 + h * 128 + 128],
                            xnf[:, dt_ * 512:(dt_ + 1) * 512], start=s, stop=p),
                            tk_wq[(dt_, h // 4)], tk_xnf_all, psfree[h % 4])
                    tk = rope_elu(ps, off, qf16, h * 512, mm, tk_cos, tk_tmp)
                    tk_tmp = tk
                    tk_qf[h] = tk
                    psfree[h % 4] = tk
            state["tk_ps0"] = merge(psfree[0], psfree[1])
            state["tk_ps1"] = merge(psfree[2], psfree[3])

            for c in range(4):
                def _zred(g, c=c):
                    for g_ in range(3):
                        g.reduce_sum(zr[:, g_:g_ + 1], kf[:, g_ * 512 + c * 128:g_ * 512 + c * 128 + 128], AX.X)
                    return g.reduce_sum(zr[:, 3:4], kf[:, 3 * 512 + c * 128:3 * 512 + c * 128 + 128], AX.X)
                tkz1 = ev("vector", _zred, merge(*tk_kf.values()), state["tk_zb"])
                tkz2 = ev("vector", lambda g: g.tensor_add(zf[:], zf[:], zr[:]), tkz1, state["tk_zf"], vself())
                state["tk_zf"] = tkz2
                stmm = []
                for g_ in range(4):
                    stmm.append(ev("tensor", lambda g, c=c, g_=g_: g.matmul(
                        ps2[:, g_ * 128:(g_ + 1) * 128],
                        kft[:, (c * 4 + g_) * 128:(c * 4 + g_) * 128 + 128],
                        vt[:, c * 512 + g_ * 128:c * 512 + g_ * 128 + 128],
                        start=True, stop=True),
                        tk_kft[(c, g_)], tk_vt[c], state["tk_ps2"]))
                tk_stmm = merge(*stmm)
                if is_prefix:
                    tka = ev("vector", lambda g: g.tensor_tensor(stf[:], stf[:], ps2[:], ALU.add),
                             tk_stmm, state["tk_stf"], vself())
                    state["tk_stf"] = tka
                    state["tk_ps2"] = tka
                    continue
                num_done = []
                den_done = []
                abf_tok = []
                for hb in range(2):
                    amms = []
                    for hh in range(8):
                        h = hb * 8 + hh
                        g_ = h // 4
                        amms.append(ev("tensor", lambda g, c=c, h=h, g_=g_, hh=hh: g.matmul(
                            ps0[:, hh * 128:(hh + 1) * 128],
                            kf[:, g_ * 512 + c * 128:g_ * 512 + c * 128 + 128],
                            qf16[:, h * 512 + c * 128:h * 512 + c * 128 + 128],
                            start=True, stop=True),
                            tk_kf[g_], tk_qf[h], state["tk_ps0"]))
                    tk_abf = ev("vector", lambda g, hb=hb: g.tensor_tensor(
                        abf[:].rearrange("p (h j) -> p h j", h=8),
                        ps0[:].rearrange("p (h j) -> p h j", h=8),
                        maskb[:].unsqueeze(1).broadcast_to([128, 8, 128]),
                        ALU.mult), merge(*amms), tk_mask,
                        merge(*(num_done + den_done)) if hb == 1 else state.get("tk_abf_last"))
                    state["tk_ps0"] = tk_abf
                    dm2 = []
                    nm2 = []
                    for hh in range(8):
                        h = hb * 8 + hh
                        g_ = h // 4
                        ev("tensor", lambda g, hh=hh, h=h: g.matmul(
                            psD[:, h:h + 1], abf[:, hh * 128:(hh + 1) * 128],
                            onesb[:], start=True, stop=False),
                            tk_abf, tk_ones, state["tk_psD"])
                        dm2.append(ev("tensor", lambda g, h=h, g_=g_, c=c: g.matmul(
                            psD[:, h:h + 1],
                            qf16[:, h * 512 + c * 128:h * 512 + c * 128 + 128],
                            zb[:, g_:g_ + 1], start=False, stop=True),
                            state["tk_zb"], tk_qf[h]))
                        ev("tensor", lambda g, hh=hh, h=h, g_=g_, c=c: g.matmul(
                            ps1[:, hh * 128:(hh + 1) * 128],
                            qf16[:, h * 512 + c * 128:h * 512 + c * 128 + 128],
                            stb[:, g_ * 128:(g_ + 1) * 128], start=True, stop=False),
                            state["tk_stb"], tk_qf[h], state["tk_ps1"])
                        nm2.append(ev("tensor", lambda g, hh=hh, g_=g_, c=c: g.matmul(
                            ps1[:, hh * 128:(hh + 1) * 128],
                            abf[:, hh * 128:(hh + 1) * 128],
                            vt[:, c * 512 + g_ * 128:c * 512 + g_ * 128 + 128],
                            start=False, stop=True),
                            tk_abf, tk_vt[c]))
                    den_done = dm2
                    num_done = nm2
                    state["tk_abf_last"] = merge(*(dm2 + nm2))
                    tkd1 = ev("vector", lambda g, hb=hb: g.tensor_scalar_add(
                        dent[:, hb * 8:(hb + 1) * 8], psD[:, hb * 8:(hb + 1) * 8], 1e-6), merge(*dm2))
                    tkd2 = ev("vector", lambda g, hb=hb: g.reciprocal(
                        rden[:, hb * 8:(hb + 1) * 8], dent[:, hb * 8:(hb + 1) * 8]), tkd1, vself())
                    state["tk_psD"] = tkd1
                    tko = ev("vector", lambda g, hb=hb: g.tensor_tensor(
                        otm[:, hb * 1024:(hb + 1) * 1024].rearrange("p (h d) -> p h d", h=8),
                        ps1[:].rearrange("p (h d) -> p h d", h=8),
                        rden[:, hb * 8:(hb + 1) * 8].unsqueeze(2).broadcast_to([128, 8, 128]),
                        ALU.mult), tkd2, merge(*nm2), state.get("tk_otm_rd"), vself())
                    state["tk_ps1"] = tko
                    if hb == 0:
                        tk_otm0 = tko
                    else:
                        tk_otm1 = tko
                tka = ev("vector", lambda g: g.tensor_tensor(stf[:], stf[:], ps2[:], ALU.add),
                         tk_stmm, state["tk_stf"], merge(*num_done), vself())
                state["tk_stf"] = tka
                state["tk_ps2"] = tka
                state["tk_stb"] = ev("scalar", lambda g: g.activation(stb[:], stf[:], ACT.Copy), tka)
                state["tk_zb"] = ev("scalar", lambda g: g.activation(zb[:], zf[:], ACT.Copy),
                                    state["tk_zf"], merge(*den_done))
                tk_otm = merge(tk_otm0, tk_otm1)
                wr = []
                for grp in range(4):
                    tts = []
                    for k in range(4):
                        h = grp * 4 + k
                        tts.append(ev("tensor", lambda g, h=h, k=k: g.transpose(
                            psT[:, k * 128:(k + 1) * 128], otm[:, h * 128:(h + 1) * 128], identb[:]),
                            tk_otm, state["tk_psT"]))
                    tev = ev("vector", lambda g: g.tensor_copy(otf[:], psT[:]),
                             merge(*tts), state.get("tk_otf_wr"))
                    state["tk_psT"] = tev
                    tcol = (sc * NCH + c) * 128
                    tkw = ev("gpsimd", lambda g, grp=grp, tcol=tcol: g.dma_start(
                        out=s_out[grp * 512:(grp + 1) * 512, tcol:tcol + 128].rearrange("(k p) t -> p k t", p=128),
                        in_=otf[:].rearrange("p (k t) -> p k t", k=4)), tev)
                    state["tk_otf_wr"] = tkw
                    wr.append(tkw)
                state["tk_otm_rd"] = merge(*tts)

        def dump_sc0():
            ev("gpsimd", lambda g: g.dma_start(out=dbg_kf, in_=kf[:]), dict(cnt))
            ev("gpsimd", lambda g: g.dma_start(out=dbg_vt, in_=vt[:]))
            ev("gpsimd", lambda g: g.dma_start(out=dbg_qf, in_=qf16[:]))
            ev("gpsimd", lambda g: g.dma_start(out=dbg_xnf, in_=xnf[:]))

        super_chunk(0, True)
        for dt_ in range(16):
            for p in range(4):
                tk_wq[(dt_, p)] = wload(si, pool1, dt_ * 2048 + p * 512,
                                        wqT[dt_ * 128:(dt_ + 1) * 128, p * 512:(p + 1) * 512]); si += 1
        for sc in range(1, NSC):
            super_chunk(sc, True)
        bar()
        tkg1 = ev("vector", lambda g: g.tensor_single_scalar(stf[:], stf[:], pfl[:], ALU.mult), tk_pfl)
        state["tk_stf"] = tkg1
        state["tk_stb"] = ev("scalar", lambda g: g.activation(stb[:], stf[:], ACT.Copy), tkg1)
        tkg2 = ev("vector", lambda g: g.tensor_single_scalar(zf[:], zf[:], pfl[:], ALU.mult), tk_pfl, vself())
        state["tk_zf"] = tkg2
        state["tk_zb"] = ev("scalar", lambda g: g.activation(zb[:], zf[:], ACT.Copy), tkg2)
        if DEBUG:
            ev("gpsimd", lambda g: g.dma_start(out=dbg_st0, in_=stb[:]), state["tk_stb"])
            tkzd = ev("gpsimd", lambda g: g.dma_start(out=dbg_zb0, in_=zb[:]), state["tk_zb"])
            state["tk_stb"] = merge(state["tk_stb"], tkzd)
        for sc in range(NSC):
            super_chunk(sc, False)
            if DEBUG and sc == 0:
                dump_sc0()
        bar()

        tk_wo = {}
        for dt_ in range(16):
            for p in range(4):
                tk_wo[(dt_, p)] = wload(si, pool1, dt_ * 2048 + p * 512,
                                        woT[dt_ * 128:(dt_ + 1) * 128, p * 512:(p + 1) * 512]); si += 1
        tk_kft_rd = None
        tk_x_rd = [None, None]
        tk_Tc_rd = merge(*tk_wo.values())
        for tt in range(16):
            d1 = ev("gpsimd", lambda g, tt=tt: g.dma_start(
                out=kft[:, 0:1024].rearrange("p (k t) -> p k t", k=8),
                in_=s_out[0:1024, tt * 128:(tt + 1) * 128].rearrange("(k p) t -> p k t", p=128)), tk_kft_rd)
            d2 = ev("gpsimd", lambda g, tt=tt: g.dma_start(
                out=kft[:, 1024:2048].rearrange("p (k t) -> p k t", k=8),
                in_=s_out[1024:2048, tt * 128:(tt + 1) * 128].rearrange("(k p) t -> p k t", p=128)), tk_kft_rd)
            tk_of = merge(d1, d2)
            xb_ = T_a if tt % 2 == 0 else T_b
            tk_xd = ev("sync", lambda g, tt=tt, xb_=xb_: g.dma_start(
                out=xb_[:], in_=x_own[tt * 128:(tt + 1) * 128, :]), tk_x_rd[tt % 2])
            evs = []
            mmlast = None
            for dmc in range(4):
                ps, off = [(ps0, 0), (ps0, 512), (ps1, 0), (ps1, 512)][dmc]
                pfree = state["tk_ps0"] if dmc < 2 else state["tk_ps1"]
                for ft in range(16):
                    mmlast = ev("tensor", lambda g, ps=ps, off=off, ft=ft, dmc=dmc, s=(ft == 0), p=(ft == 15): g.matmul(
                        ps[:, off:off + 512], kft[:, ft * 128:(ft + 1) * 128],
                        pool1[:, ft * 2048 + dmc * 512:ft * 2048 + dmc * 512 + 512], start=s, stop=p),
                        tk_of, tk_wo[(ft, dmc)], pfree)
                tke = ev("vector", lambda g, ps=ps, off=off, dmc=dmc, xb_=xb_: g.tensor_tensor(
                    T_c[:, dmc * 512:(dmc + 1) * 512], ps[:, off:off + 512],
                    xb_[:, dmc * 512:(dmc + 1) * 512], ALU.add), mmlast, tk_xd, tk_Tc_rd)
                evs.append(tke)
                if dmc < 2:
                    state["tk_ps0"] = tke
                else:
                    state["tk_ps1"] = tke
            tk_kft_rd = mmlast
            tk_x_rd[tt % 2] = merge(*evs)
            tk_Tc_rd = ev("gpsimd", lambda g, tt=tt: g.dma_start(
                out=s_x1[tt * 128:(tt + 1) * 128, :], in_=T_c[:]), merge(*evs))
        bar()

        ps_tiles = []
        for i in range(16):
            ps_tiles.append((wupT[i * 128:(i + 1) * 128, 0:2048], 0))
            ps_tiles.append((wupT[i * 128:(i + 1) * 128, 2048:4096], 0))
        for i in range(32):
            ps_tiles.append((wdnT[i * 128:(i + 1) * 128, :], 1))
        C6 = float(1.5 * 2 ** 23 * 2 ** -6)
        slot_rd = [None, None]
        slot_dma = [None, None]
        tk_accw = None
        sbuf2 = [T_a, T_b]
        tmps = [tmp1, tmp2]
        for i, (ap, col) in enumerate(ps_tiles):
            sl = i % 2
            tkd = ev("sync", lambda g, ap=ap, sl=sl: g.dma_start(out=sbuf2[sl][:], in_=ap), slot_rd[sl])
            tkr = ev("vector", lambda g, sl=sl: g.reduce_sum(tmps[sl][:, 0:1], sbuf2[sl][:], AX.X, apply_absolute_value=True), tkd, slot_dma[sl])
            slot_rd[sl] = tkr
            t_hi = ev("vector", lambda g, sl=sl: g.tensor_scalar(tmps[sl][:, 1:2], tmps[sl][:, 0:1], C6, C6, ALU.add, ALU.subtract), tkr, vself())
            t_lo = ev("vector", lambda g, sl=sl: g.tensor_sub(tmps[sl][:, 0:1], tmps[sl][:, 0:1], tmps[sl][:, 1:2]), t_hi, vself())
            tka = ev("vector", lambda g, sl=sl, col=col: g.tensor_tensor(
                acc[:, col:col + 1], acc[:, col:col + 1], tmps[sl][:, 1:2], ALU.add), t_hi, tk_accw, vself())
            tka2 = ev("vector", lambda g, sl=sl, col=col: g.tensor_tensor(
                acc[:, 2 + col:3 + col], acc[:, 2 + col:3 + col], tmps[sl][:, 0:1], ALU.add), t_lo, tka, vself())
            slot_dma[sl] = tka2
            tk_accw = tka2
        tko3 = ev("vector", lambda g: g.memset(T_c[:, 0:128], 1.0), merge(*[t for t in [slot_dma[0], slot_dma[1]] if t]))
        tkmm = ev("tensor", lambda g: g.matmul(psD[:, 0:4], T_c[:, 0:128], acc[:],
                                               start=True, stop=True), tko3, tk_accw, state["tk_psD"])
        tkc0 = ev("vector", lambda g: g.tensor_copy(stats[:, 0:4], psD[:, 0:4]), tkmm)
        tkf0 = ev("vector", lambda g: g.tensor_tensor(scl[:], stats[:, 0:2], stats[:, 2:4], ALU.add), tkc0, vself())
        tkf = ev("vector", lambda g: g.tensor_scalar(scl[:], scl[:], float(2.0 ** -23), 1e-5, ALU.mult, ALU.max), tkf0, vself())
        state["tk_psD"] = tkf
        tkh = ev("vector", lambda g: g.tensor_scalar_mul(hsc[:], scl[:], 0.5), tkf, vself())
        tkn = ev("vector", lambda g: g.tensor_scalar_mul(nhs[:], scl[:], -0.5), tkf, vself())
        bar()

        tk_aq1 = {}
        prevload = None
        prev_done = None
        for tt in range(16):
            tkd = ev("sync", lambda g, tt=tt: g.dma_start(out=T_a[:], in_=s_x1[tt * 128:(tt + 1) * 128, :]), prevload)
            t1 = ev("vector", lambda g: g.reduce_sum(stats[:, 0:1], T_a[:], AX.X), tkd, prev_done)
            tq = ev("scalar", lambda g: g.activation(T_b[:], T_a[:], ACT.Square), tkd, prev_done)
            t2 = ev("vector", lambda g: g.reduce_sum(stats[:, 1:2], T_b[:], AX.X), tq)
            tm = ev("vector", lambda g: g.tensor_scalar_mul(stats[:, 0:1], stats[:, 0:1], 1.0 / DM), t1, vself())
            tqq = ev("vector", lambda g: g.tensor_scalar_mul(stats[:, 1:2], stats[:, 1:2], 1.0 / DM), t2, vself())
            tv = ev("vector", lambda g: g.tensor_tensor(stats[:, 2:3], stats[:, 0:1], stats[:, 0:1], ALU.mult), tm, vself())
            tv2 = ev("vector", lambda g: g.tensor_sub(stats[:, 1:2], stats[:, 1:2], stats[:, 2:3]), tqq, tv, vself())
            tsd = ev("scalar", lambda g: g.activation(stats[:, 2:3], stats[:, 1:2], ACT.Sqrt, bias=np.float32(1e-5)), tv2)
            tr = ev("vector", lambda g: g.reciprocal(stats[:, 3:4], stats[:, 2:3]), tsd)
            tt1 = ev("vector", lambda g: g.tensor_single_scalar(T_c[:], T_a[:], stats[:, 0:1], ALU.subtract), tm, tkd)
            tam = ev("vector", lambda g: g.reduce_max(stats[:, 4:5], T_c[:], AX.X, apply_absolute_value=True), tt1, vself())
            tl = ev("vector", lambda g: g.tensor_tensor(stats[:, 5:6], stats[:, 4:5], stats[:, 3:4], ALU.mult), tam, tr, vself())
            tl = ev("vector", lambda g: g.tensor_scalar_max(stats[:, 5:6], stats[:, 5:6], 1e-5), tl, vself())
            tl2 = ev("vector", lambda g: g.reciprocal(stats[:, 6:7], stats[:, 5:6]), tl, vself())
            tl3 = ev("vector", lambda g: g.tensor_tensor(stats[:, 7:8], stats[:, 3:4], stats[:, 6:7], ALU.mult), tl2, vself())
            tl3 = ev("vector", lambda g: g.tensor_scalar_mul(stats[:, 7:8], stats[:, 7:8], 127.0), tl3, vself())
            tc1 = ev("vector", lambda g, tt=tt: g.tensor_tensor(cfac[:, tt:tt + 1], scl[:, 0:1], stats[:, 5:6], ALU.mult), tl, vself())
            tc1 = ev("vector", lambda g, tt=tt: g.tensor_scalar_mul(cfac[:, tt:tt + 1], cfac[:, tt:tt + 1], 1.0 / 127.0), tc1, vself())
            ty = ev("vector", lambda g: g.tensor_single_scalar(T_c[:], T_c[:], stats[:, 7:8], ALU.mult), tl3, tam, vself())
            ty = ev("vector", lambda g: g.tensor_scalar(T_c[:], T_c[:], CLIP, -CLIP, ALU.min, ALU.max), ty, vself())
            ta_ = ev("vector", lambda g: g.tensor_scalar(T_b[:], T_c[:], C23, C23, ALU.add, ALU.subtract), ty, vself())
            tc_ = ev("vector", lambda g: g.tensor_scalar(T_c[:], T_b[:], C52, C52, ALU.add, ALU.subtract), ta_, vself())
            ts_ = ev("vector", lambda g: g.tensor_sub(T_c[:], T_c[:], T_b[:]), tc_, vself())
            taq = ev("vector", lambda g: g.scalar_tensor_tensor(sqt[:], T_c[:], fdom[:], T_b[:], ALU.mult, ALU.add), ts_, tk_fdom, t2, vself())
            trans = []
            for grp in range(4):
                tts = []
                for k in range(4):
                    dt_ = grp * 4 + k
                    tts.append(ev("tensor", lambda g, dt_=dt_, k=k: g.transpose(
                        psT[:, k * 128:(k + 1) * 128], sqt[:, dt_ * 128:(dt_ + 1) * 128], identb[:]),
                        taq, state["tk_psT"]))
                def _eva(g, grp=grp, tt=tt):
                    for k in range(3):
                        g.tensor_copy(pool1[:, (grp * 4 + k) * 2048 + tt * 128:(grp * 4 + k) * 2048 + tt * 128 + 128],
                                      psT[:, k * 128:(k + 1) * 128])
                    return g.tensor_copy(pool1[:, (grp * 4 + 3) * 2048 + tt * 128:(grp * 4 + 3) * 2048 + tt * 128 + 128],
                                         psT[:, 384:512])
                tev = ev("vector", _eva, merge(*tts))
                state["tk_psT"] = tev
                trans.append(tev)
            tk_aq1[tt] = merge(*trans)
            prevload = merge(tt1, tq, t1)
            prev_done = merge(*trans)
            if tt % 2 == 1:
                bar()
        bar()

        tk_ru_wr = []
        tk_wup_rd = None
        tk_rum = {t: None for t in range(16)}
        slotC_rd = [None, None, None, None]
        slotA_rd = [None, None, None, None]
        for oc in range(8):
            tkq_tiles = []
            for dt_ in range(16):
                a = (dt_ % 4) * 512
                tkd = ev("sync", lambda g, dt_=dt_, oc=oc, a=a: g.dma_start(
                    out=T_a[:, a:a + 512], in_=wupT[dt_ * 128:(dt_ + 1) * 128, oc * 512:(oc + 1) * 512]),
                    slotA_rd[dt_ % 4])
                tkl = ev("vector", lambda g, a=a: g.tensor_single_scalar(T_b[:, a:a + 512], T_a[:, a:a + 512], nhs[:, 0:1], ALU.is_lt), tkd)
                tkq = ev("vector", lambda g, dt_=dt_, a=a: g.scalar_tensor_tensor(
                    qf16[:, dt_ * 512:(dt_ + 1) * 512], T_a[:, a:a + 512], hsc[:, 0:1], T_b[:, a:a + 512],
                    ALU.is_gt, ALU.subtract), tkl, tk_wup_rd, vself())
                tkq_tiles.append(tkq)
                slotA_rd[dt_ % 4] = tkq
            tk_wupq = merge(*tkq_tiles)
            mm_all = []
            for tt in range(16):
                ps, off = [(ps0, 0), (ps0, 512), (ps1, 0), (ps1, 512)][tt % 4]
                pfree = state["tk_ps0"] if tt % 4 < 2 else state["tk_ps1"]
                mm = None
                for dt_ in range(16):
                    mm = ev("tensor", lambda g, ps=ps, off=off, dt_=dt_, tt=tt, s=(dt_ == 0), p=(dt_ == 15): g.matmul(
                        ps[:, off:off + 512],
                        pool1[:, dt_ * 2048 + tt * 128:dt_ * 2048 + tt * 128 + 128],
                        qf16[:, dt_ * 512:(dt_ + 1) * 512], start=s, stop=p),
                        tk_aq1[tt], tk_wupq, pfree)
                mm_all.append(mm)
                slot = tt % 4
                tkru = ev("vector", lambda g, ps=ps, off=off, slot=slot: g.tensor_scalar_max(
                    T_c[:, slot * 512:(slot + 1) * 512], ps[:, off:off + 512], 0.0), mm, slotC_rd[slot])
                if tt % 4 < 2:
                    state["tk_ps0"] = tkru
                else:
                    state["tk_ps1"] = tkru
                tkrm = ev("vector", lambda g, slot=slot: g.reduce_max(stats[:, 8 + slot:9 + slot], T_c[:, slot * 512:(slot + 1) * 512], AX.X), tkru, vself())
                tkac = ev("vector", lambda g, tt=tt, slot=slot: g.tensor_tensor(
                    rumac[:, tt:tt + 1], rumac[:, tt:tt + 1], stats[:, 8 + slot:9 + slot], ALU.max), tkrm, tk_rum[tt], vself())
                tk_rum[tt] = tkac
                tkw = ev("gpsimd", lambda g, tt=tt, oc=oc, slot=slot: g.dma_start(
                    out=s_ru[tt * 128:(tt + 1) * 128, oc * 512:(oc + 1) * 512],
                    in_=T_c[:, slot * 512:(slot + 1) * 512]), tkru)
                slotC_rd[slot] = tkw
                tk_ru_wr.append(tkw)
            tk_wup_rd = merge(*mm_all)
            bar()
        bar()

        tk_a_rd = None
        tk_b_rd = None
        tk_sq_rd = None
        for i in range(32):
            tkd = ev("sync", lambda g, i=i: g.dma_start(out=T_a[:], in_=wdnT[i * 128:(i + 1) * 128, :]), tk_a_rd)
            tkl = ev("vector", lambda g: g.tensor_single_scalar(T_b[:], T_a[:], nhs[:, 1:2], ALU.is_lt), tkd, tk_b_rd)
            tkq = ev("vector", lambda g: g.scalar_tensor_tensor(sqt[:], T_a[:], hsc[:, 1:2], T_b[:], ALU.is_gt, ALU.subtract), tkl, tk_sq_rd, vself())
            tk_a_rd = tkq
            tk_b_rd = tkq
            tkw = ev("gpsimd", lambda g, i=i: g.dma_start(out=s_wdn[i * 128:(i + 1) * 128, :], in_=sqt[:]), tkq)
            tk_sq_rd = tkw
            if i % 4 == 3:
                bar()
        bar()

        for half in range(2):
            tk_aq2 = {}
            prevl = None
            prevd = None
            for ti in range(8):
                tt = half * 8 + ti
                d1 = ev("sync", lambda g, tt=tt: g.dma_start(out=T_a[:], in_=s_ru[tt * 128:(tt + 1) * 128, 0:2048]), prevl)
                d2 = ev("sync", lambda g, tt=tt: g.dma_start(out=T_b[:], in_=s_ru[tt * 128:(tt + 1) * 128, 2048:4096]), prevl)
                t1 = ev("vector", lambda g, tt=tt: g.tensor_tensor(stats[:, 0:1], rumac[:, tt:tt + 1], cfac[:, tt:tt + 1], ALU.mult), prevd)
                t2 = ev("vector", lambda g: g.tensor_tensor(stats[:, 1:2], stats[:, 0:1], stats[:, 0:1], ALU.mult), t1, vself())
                t3 = ev("vector", lambda g: g.tensor_scalar_max(stats[:, 1:2], stats[:, 1:2], 1e-5), t2, vself())
                t4 = ev("vector", lambda g, tt=tt: g.tensor_tensor(stats[:, 2:3], cfac[:, tt:tt + 1], cfac[:, tt:tt + 1], ALU.mult), prevd)
                t5 = ev("vector", lambda g: g.reciprocal(stats[:, 3:4], stats[:, 1:2]), t3)
                t6 = ev("vector", lambda g: g.tensor_tensor(stats[:, 2:3], stats[:, 2:3], stats[:, 3:4], ALU.mult), t4, t5, vself())
                t7 = ev("vector", lambda g: g.tensor_scalar_mul(stats[:, 2:3], stats[:, 2:3], 127.0), t6, vself())
                t8 = ev("scalar", lambda g, tt=tt: g.activation(sq2s[:, tt:tt + 1], stats[:, 2:3], ACT.Sqrt), t7)
                t9 = ev("vector", lambda g, tt=tt: g.tensor_tensor(resc[:, tt:tt + 1], scl[:, 1:2], stats[:, 1:2], ALU.mult), t3, vself())
                t9 = ev("vector", lambda g, tt=tt: g.tensor_scalar_mul(resc[:, tt:tt + 1], resc[:, tt:tt + 1], 1.0 / 127.0), t9, vself())
                aq_done = []
                for hp, buf, dtk in ((0, T_a, d1), (1, T_b, d2)):
                    u1 = ev("vector", lambda g, buf=buf, tt=tt: g.tensor_single_scalar(buf[:], buf[:], sq2s[:, tt:tt + 1], ALU.mult), t8, dtk)
                    u2 = ev("scalar", lambda g, buf=buf: g.activation(buf[:], buf[:], ACT.Square), u1)
                    u3 = ev("vector", lambda g, buf=buf: g.tensor_scalar(T_c[:], buf[:], C23, C23, ALU.add, ALU.subtract), u2, slotC_rd[0])
                    u4 = ev("vector", lambda g, buf=buf: g.tensor_scalar(buf[:], T_c[:], C52, C52, ALU.add, ALU.subtract), u3, vself())
                    u5 = ev("vector", lambda g, buf=buf: g.tensor_sub(buf[:], buf[:], T_c[:]), u4, vself())
                    u6 = ev("vector", lambda g, buf=buf, hp=hp: g.scalar_tensor_tensor(
                        xnf[:, hp * 2048:(hp + 1) * 2048], buf[:], fdom[:], T_c[:], ALU.mult, ALU.add), u5, vself())
                    slotC_rd[0] = u6
                    aq_done.append(u6)
                tk_aq2tm = merge(*aq_done)
                trans = []
                for grp in range(8):
                    tts = []
                    for k in range(4):
                        ot = grp * 4 + k
                        tts.append(ev("tensor", lambda g, ot=ot, k=k: g.transpose(
                            psT[:, k * 128:(k + 1) * 128], xnf[:, ot * 128:(ot + 1) * 128], identb[:]),
                            tk_aq2tm, state["tk_psT"]))
                    def _evb(g, grp=grp, ti=ti):
                        for k in range(3):
                            g.tensor_copy(pool1[:, (grp * 4 + k) * 1024 + ti * 128:(grp * 4 + k) * 1024 + ti * 128 + 128],
                                          psT[:, k * 128:(k + 1) * 128])
                        return g.tensor_copy(pool1[:, (grp * 4 + 3) * 1024 + ti * 128:(grp * 4 + 3) * 1024 + ti * 128 + 128],
                                             psT[:, 384:512])
                    tev = ev("vector", _evb, merge(*tts))
                    state["tk_psT"] = tev
                    trans.append(tev)
                tk_aq2[ti] = merge(*trans)
                prevl = merge(aq_done[0], aq_done[1])
                prevd = merge(*trans)
                bar()
            bar()
            tk_wdn_rd = None
            tk_x1_rd = [None, None]
            tk_y_rd = [None, None]
            for dmc in range(4):
                wdl = []
                for g4 in range(4):
                    wdl.append(ev("sync", lambda g, g4=g4, dmc=dmc: g.dma_start(
                        out=poolW[:, g4 * 4096:(g4 + 1) * 4096].rearrange("p (o d) -> p o d", o=8),
                        in_=s_wdn[g4 * 1024:(g4 + 1) * 1024, dmc * 512:(dmc + 1) * 512].rearrange("(o p) d -> p o d", p=128)),
                        tk_wdn_rd))
                tk_wd = merge(*wdl)
                mm_all = []
                for ti in range(8):
                    tt = half * 8 + ti
                    ps, off = [(ps0, 0), (ps0, 512)][ti % 2]
                    pfree = state["tk_ps0"]
                    xsl = ti % 2
                    tkx = ev("sync", lambda g, tt=tt, dmc=dmc, xsl=xsl: g.dma_start(
                        out=T_a[:, xsl * 512:(xsl + 1) * 512] if False else T_a[:, xsl * 512:(xsl + 1) * 512],
                        in_=s_x1[tt * 128:(tt + 1) * 128, dmc * 512:(dmc + 1) * 512]), tk_x1_rd[xsl])
                    mm = None
                    for ot in range(32):
                        mm = ev("tensor", lambda g, ps=ps, off=off, ot=ot, ti=ti, s=(ot == 0), p=(ot == 31): g.matmul(
                            ps[:, off:off + 512],
                            pool1[:, ot * 1024 + ti * 128:ot * 1024 + ti * 128 + 128],
                            poolW[:, ot * 512:(ot + 1) * 512], start=s, stop=p),
                            tk_aq2[ti], tk_wd, pfree)
                    mm_all.append(mm)
                    tky = ev("vector", lambda g, ps=ps, off=off, tt=tt, xsl=xsl: g.scalar_tensor_tensor(
                        T_b[:, xsl * 512:(xsl + 1) * 512], ps[:, off:off + 512], resc[:, tt:tt + 1],
                        T_a[:, xsl * 512:(xsl + 1) * 512], ALU.mult, ALU.add), mm, tkx, tk_y_rd[xsl])
                    state["tk_ps0"] = tky
                    tk_x1_rd[xsl] = tky
                    tkw = ev("gpsimd", lambda g, tt=tt, dmc=dmc, xsl=xsl: g.dma_start(
                        out=y_out[tt * 128:(tt + 1) * 128, dmc * 512:(dmc + 1) * 512],
                        in_=T_b[:, xsl * 512:(xsl + 1) * 512]), tky)
                    tk_y_rd[xsl] = tkw
                tk_wdn_rd = merge(*mm_all)
                bar()
            bar()

        if DEBUG:
            ev("gpsimd", lambda g: g.dma_start(out=dbg_out, in_=s_out))
            ev("gpsimd", lambda g: g.dma_start(out=dbg_x1, in_=s_x1))
            for _i in range(4):
                ev("gpsimd", lambda g, _i=_i: g.dma_start(out=dbg_ru[_i * 128:(_i + 1) * 128, :], in_=s_ru[_i * 128:(_i + 1) * 128, :]))
            for _i in range(8):
                ev("gpsimd", lambda g, _i=_i: g.dma_start(out=dbg_wdnq[_i * 512:(_i + 1) * 512, :], in_=s_wdn[_i * 512:(_i + 1) * 512, :]))
            ev("gpsimd", lambda g: g.dma_start(out=dbg_sc[:, 0:16], in_=rumac[:]))
            ev("gpsimd", lambda g: g.dma_start(out=dbg_sc[:, 16:32], in_=cfac[:]))
            ev("gpsimd", lambda g: g.dma_start(out=dbg_sc[:, 32:48], in_=resc[:]))
            ev("gpsimd", lambda g: g.dma_start(out=dbg_sc[:, 48:50], in_=scl[:]))
        final = dict(cnt)

        def runner(e):
            def _(eng):
                lw = {}
                for kind, w, fn in EV[e]:
                    for s_, v in sorted(w.items()):
                        if lw.get(s_, -1) < v:
                            eng.wait_ge(SEM[s_], v)
                            lw[s_] = v
                    if kind == "op":
                        fn(eng).then_inc(SEM[e], INC[e])
                for s_ in ENGINES:
                    if s_ != e and lw.get(s_, -1) < final[s_]:
                        eng.wait_ge(SEM[s_], final[s_])
            return _

        block.sync(runner("sync"))
        block.gpsimd(runner("gpsimd"))
        block.vector(runner("vector"))
        block.scalar(runner("scalar"))
        block.tensor(runner("tensor"))
    return nc


_cos_sin = None


def _tables():
    global _cos_sin
    if _cos_sin is None:
        inv = 1.0 / (10000.0 ** (np.arange(0, HD, 2, dtype=np.float64) / HD))
        t = np.arange(S, dtype=np.float64)
        fr = np.outer(t, inv)
        emb = np.concatenate([fr, fr], -1)
        _cos_sin = (np.cos(emb).T.astype(np.float32).copy(),
                    np.sin(emb).T.astype(np.float32).copy())
    return _cos_sin


def kernel(L0_attn_x, L0_attn_Wq, L0_attn_Wk, L0_attn_Wv, L0_attn_Wo,
           L0_ffn_w_up, L0_ffn_w_down):
    X = np.asarray(L0_attn_x, np.float32)
    wqT = np.ascontiguousarray(np.asarray(L0_attn_Wq, np.float32).T)
    wkT = np.ascontiguousarray(np.asarray(L0_attn_Wk, np.float32).T)
    wvT = np.ascontiguousarray(np.asarray(L0_attn_Wv, np.float32).T)
    woT = np.ascontiguousarray(np.asarray(L0_attn_Wo, np.float32).T)
    wupT = np.ascontiguousarray(np.asarray(L0_ffn_w_up, np.float32).T)
    wdnT = np.ascontiguousarray(np.asarray(L0_ffn_w_down, np.float32).T)
    cosF, sinF = _tables()
    maskT = np.triu(np.ones((128, 128), np.float32))
    identf = np.eye(128, dtype=np.float32)
    zeros = np.zeros((T, DM), np.float32)
    nc = build_program()
    in_maps = []
    for core in range(8):
        b, half = core // 2, core % 2
        off = half * T
        in_maps.append({
            "x_own": np.ascontiguousarray(X[b, off:off + T]),
            "x_pre": np.ascontiguousarray(X[b, 0:T]) if half else zeros,
            "pflag": np.full((128, 1), float(half), np.float32),
            "wqT": wqT, "wkT": wkT, "wvT": wvT, "woT": woT,
            "wupT": wupT, "wdnT": wdnT,
            "cos_own": np.ascontiguousarray(cosF[:, off:off + T]),
            "sin_own": np.ascontiguousarray(sinF[:, off:off + T]),
            "cos_pre": np.ascontiguousarray(cosF[:, 0:T]),
            "sin_pre": np.ascontiguousarray(sinF[:, 0:T]),
            "maskT": maskT, "identf": identf,
        })
    res = run_bass_kernel_spmd(nc, in_maps, core_ids=list(range(8)))
    out = np.empty((B, S, DM), np.float32)
    for core in range(8):
        b, half = core // 2, core % 2
        out[b, half * T:(half + 1) * T] = res.results[core]["y"]
    return out


# version 06 (final)
# speedup vs baseline: 1.0210x; 1.0210x over previous
import numpy as np
import ml_dtypes
import concourse.bass as bass
import concourse.mybir as mybir
from concourse.bass_utils import run_bass_kernel_spmd
from contextlib import ExitStack

F32 = mybir.dt.float32
BF16 = mybir.dt.bfloat16
ALU = mybir.AluOpType
ACT = mybir.ActivationFunctionType
AX = mybir.AxisListType

B, S, DM = 4, 4096, 2048
T = 2048
H, KVH, HD = 16, 4, 128
IFF = 4096
NSC, SCT, NCH = 4, 512, 4
C23 = float(1.5 * 2 ** 23)
C52 = float(1.5 * 2 ** 52)
CLIP = 127.499

ENGINES = ("sync", "gpsimd", "vector", "scalar", "tensor")
INC = {"sync": 16, "gpsimd": 16, "vector": 1, "scalar": 1, "tensor": 1}


def merge(*toks):
    out = {}
    for t in toks:
        if not t:
            continue
        for k, v in t.items():
            if out.get(k, -1) < v:
                out[k] = v
    return out


DEBUG = False


def build_program():
    nc = bass.Bass("TRN2", target_bir_lowering=False)

    def din(name, shape, dt=F32):
        return nc.dram_tensor(name, shape, dt, kind="ExternalInput").ap()

    x_own = din("x_own", [T, DM])
    x_pre = din("x_pre", [T, DM])
    pflag = din("pflag", [128, 1])
    wqT = din("wqT", [DM, DM])
    wkT = din("wkT", [DM, 512])
    wvT = din("wvT", [DM, 512])
    woT = din("woT", [DM, DM])
    wupT = din("wupT", [DM, IFF])
    wdnT = din("wdnT", [IFF, DM])
    cos_own = din("cos_own", [128, T])
    sin_own = din("sin_own", [128, T])
    cos_pre = din("cos_pre", [128, T])
    sin_pre = din("sin_pre", [128, T])
    maskT = din("maskT", [128, 128])
    identf = din("identf", [128, 128])
    y_out = nc.dram_tensor("y", [T, DM], F32, kind="ExternalOutput").ap()
    s_out = nc.dram_tensor("s_out", [DM, T], BF16, kind="Internal").ap()
    s_x1 = nc.dram_tensor("s_x1", [T, DM], F32, kind="Internal").ap()
    s_ru = nc.dram_tensor("s_ru", [T, IFF], F32, kind="Internal").ap()
    s_wdn = nc.dram_tensor("s_wdn", [IFF, DM], BF16, kind="Internal").ap()
    if DEBUG:
        dbg_out = nc.dram_tensor("dbg_out", [DM, T], BF16, kind="ExternalOutput").ap()
        dbg_x1 = nc.dram_tensor("dbg_x1", [T, DM], F32, kind="ExternalOutput").ap()
        dbg_kf = nc.dram_tensor("dbg_kf", [128, 2048], BF16, kind="ExternalOutput").ap()
        dbg_vt = nc.dram_tensor("dbg_vt", [128, 2048], BF16, kind="ExternalOutput").ap()
        dbg_qf = nc.dram_tensor("dbg_qf", [128, 8192], BF16, kind="ExternalOutput").ap()
        dbg_xnf = nc.dram_tensor("dbg_xnf", [128, 8192], BF16, kind="ExternalOutput").ap()
        dbg_st0 = nc.dram_tensor("dbg_st0", [128, 512], BF16, kind="ExternalOutput").ap()
        dbg_zb0 = nc.dram_tensor("dbg_zb0", [128, 4], BF16, kind="ExternalOutput").ap()
        dbg_ru = nc.dram_tensor("dbg_ru", [512, IFF], F32, kind="ExternalOutput").ap()
        dbg_wdnq = nc.dram_tensor("dbg_wdnq", [IFF, DM], BF16, kind="ExternalOutput").ap()
        dbg_sc = nc.dram_tensor("dbg_sc", [128, 52], F32, kind="ExternalOutput").ap()

    with ExitStack() as st:
        sb = lambda n, s, d=F32: st.enter_context(nc.sbuf_tensor(n, s, d))
        pool1 = sb("pool1", [128, 16 * 2048], BF16)
        poolW = sb("poolW", [128, 16384], BF16)
        qf16 = sb("qf16", [128, 8192], BF16)
        xnf = sb("xnf", [128, 8192], BF16)
        T_a = sb("T_a", [128, 2048])
        T_b = sb("T_b", [128, 2048])
        T_c = sb("T_c", [128, 2048])
        sqt = sb("sqt", [128, 2048], BF16)
        kf = sb("kf", [128, 2048], BF16)
        kft = sb("kft", [128, 2048], BF16)
        vt = sb("vt", [128, 2048], BF16)
        r1 = sb("r1", [128, 512])
        r2 = sb("r2", [128, 512])
        stf = sb("stf", [128, 512])
        stb = sb("stb", [128, 512], BF16)
        abf = sb("abf", [128, 1024], BF16)
        otm = sb("otm", [128, 2048], BF16)
        otf = sb("otf", [128, 512], BF16)
        cosb = sb("cosb", [128, 512])
        sinb = sb("sinb", [128, 512])
        maskb = sb("maskb", [128, 128])
        identb = sb("identb", [128, 128], BF16)
        onesb = sb("onesb", [128, 1], BF16)
        pfl = sb("pfl", [128, 1])
        fdom = sb("fdom", [128, 1])
        zf = sb("zf", [128, 4])
        zb = sb("zb", [128, 4], BF16)
        zr = sb("zr", [128, 4])
        dent = sb("dent", [128, 16])
        rden = sb("rden", [128, 16])
        stats = sb("stats", [128, 32])
        acc = sb("acc", [128, 4])
        tmp1 = sb("tmp1", [128, 2])
        tmp2 = sb("tmp2", [128, 2])
        tmpf = sb("tmpf", [64, 2])
        scl = sb("scl", [128, 2])
        hsc = sb("hsc", [128, 2])
        nhs = sb("nhs", [128, 2])
        cfac = sb("cfac", [128, 16])
        rumac = sb("rumac", [128, 16])
        resc = sb("resc", [128, 16])
        sq2s = sb("sq2s", [128, 16])

        ps0 = st.enter_context(nc.psum_tensor("ps0", [128, 1024], F32))
        ps1 = st.enter_context(nc.psum_tensor("ps1", [128, 1024], F32))
        ps2 = st.enter_context(nc.psum_tensor("ps2", [128, 512], F32))
        psD = st.enter_context(nc.psum_tensor("psD", [128, 16], F32))
        psT = st.enter_context(nc.psum_tensor("psT", [128, 512], BF16))
        SEM = {e: st.enter_context(nc.semaphore(f"sem_{e}")) for e in ENGINES}
        block = st.enter_context(nc.Block())

        EV = {e: [] for e in ENGINES}
        cnt = {e: 0 for e in ENGINES}

        def ev(e, fn, *waits):
            w = merge(*waits)
            cnt[e] += INC[e]
            EV[e].append(("op", w, fn))
            return {e: cnt[e]}

        def vself():
            return {"vector": cnt["vector"]}

        def bar():
            snap = dict(cnt)
            for e in ENGINES:
                w = {k: v for k, v in snap.items() if k != e}
                EV[e].append(("bar", w, None))

        tk_mask = ev("sync", lambda g: g.dma_start(out=maskb[:], in_=maskT))
        tk_id = ev("sync", lambda g: g.dma_start(out=T_c[:, 0:128], in_=identf))
        tk_pfl = ev("sync", lambda g: g.dma_start(out=pfl[:], in_=pflag))
        tk_idb = ev("vector", lambda g: g.tensor_copy(identb[:], T_c[:, 0:128]), tk_id)
        tk_ones = ev("vector", lambda g: g.memset(onesb[:], 1.0))
        t0 = ev("vector", lambda g: g.memset(fdom[:], 1.0))
        t0 = ev("vector", lambda g: g.tensor_scalar(fdom[:], fdom[:], float(2 ** -40), 1.0, ALU.add, ALU.subtract), t0, vself())
        tk_fdom = ev("vector", lambda g: g.tensor_scalar_mul(fdom[:], fdom[:], float(2 ** 40)), t0, vself())
        ev("vector", lambda g: g.memset(acc[:], 0.0))
        ev("vector", lambda g: g.memset(zf[:], 0.0))
        ev("vector", lambda g: g.memset(rumac[:], 0.0))
        ev("vector", lambda g: g.memset(stf[:], 0.0))

        stage_free = [None, None, None, None]

        def wload(si, dst, dc, src_ap, w=512):
            slot = si % 4
            a, b = slot * 512, slot * 512 + w
            tk_d = ev("sync", lambda g: g.dma_start(out=T_c[:, a:b], in_=src_ap), stage_free[slot])
            tk_c = ev("scalar", lambda g: g.activation(dst[:, dc:dc + w], T_c[:, a:b], ACT.Copy), tk_d)
            stage_free[slot] = tk_c
            return tk_c

        si = 0
        tk_wk, tk_wv, tk_wq = {}, {}, {}
        for dt_ in range(16):
            tk_wk[dt_] = wload(si, poolW, dt_ * 512, wkT[dt_ * 128:(dt_ + 1) * 128, :]); si += 1
        for dt_ in range(16):
            tk_wv[dt_] = wload(si, poolW, 8192 + dt_ * 512, wvT[dt_ * 128:(dt_ + 1) * 128, :]); si += 1

        state = {
            "tk_stb": None, "tk_zb": None, "tk_stf": None, "tk_zf": None,
            "tk_ps0": None, "tk_ps1": None, "tk_ps2": None, "tk_psD": None,
            "tk_psT": None,
        }

        def rope_elu(ps, off, dst, dcol, tk_ps, tk_cos, tk_prev_tmp):
            w0 = merge(tk_ps, tk_cos, tk_prev_tmp)
            ta = ev("vector", lambda g: g.tensor_scalar_mul(r1[0:64, :], ps[64:128, off:off + 512], -1.0), w0)
            tb = ev("vector", lambda g: g.tensor_scalar_mul(r1[64:128, :], ps[0:64, off:off + 512], 1.0), w0)
            tc2 = ev("vector", lambda g: g.tensor_tensor(r2[:], ps[:, off:off + 512], cosb[:], ALU.mult), w0)
            td = ev("vector", lambda g: g.tensor_tensor(r1[:], r1[:], sinb[:], ALU.mult), ta, tb, vself())
            te = ev("vector", lambda g: g.tensor_tensor(r2[:], r2[:], r1[:], ALU.add), tc2, td, vself())
            tf = ev("vector", lambda g: g.tensor_scalar_min(r1[:], r2[:], 0.0), te, vself())
            tg = ev("scalar", lambda g: g.activation(r1[:], r1[:], ACT.Exp), tf)
            th = ev("vector", lambda g: g.scalar_tensor_tensor(dst[:, dcol:dcol + 512], r2[:], 0.0, r1[:], ALU.max, ALU.add), tg)
            return th

        def super_chunk(sc, is_prefix):
            bar()
            xsrc = x_pre if is_prefix else x_own
            cs, ss = (cos_pre, sin_pre) if is_prefix else (cos_own, sin_own)
            tkc = ev("sync", lambda g: g.dma_start(out=cosb[:], in_=cs[:, sc * SCT:(sc + 1) * SCT]))
            tks = ev("sync", lambda g: g.dma_start(out=sinb[:], in_=ss[:, sc * SCT:(sc + 1) * SCT]))
            tk_cos = merge(tkc, tks)
            xb = [T_a, T_b, T_a, T_b]
            tk_x = [None] * 4
            tk_sq = None
            tk_xfree = [None, None]
            tk_xnf = []
            for c in range(4):
                tk_x[c] = ev("sync", lambda g, c=c: g.dma_start(
                    out=xb[c][:], in_=xsrc[(sc * NCH + c) * 128:(sc * NCH + c) * 128 + 128, :]), tk_xfree[c % 2])
                sc0 = c * 8
                t1 = ev("vector", lambda g, c=c, sc0=sc0: g.reduce_sum(stats[:, sc0:sc0 + 1], xb[c][:], AX.X), tk_x[c])
                tq = ev("scalar", lambda g, c=c: g.activation(sqt[:], xb[c][:], ACT.Square), tk_x[c], tk_sq)
                t2 = ev("vector", lambda g, sc0=sc0: g.reduce_sum(stats[:, sc0 + 1:sc0 + 2], sqt[:], AX.X), tq)
                tm = ev("vector", lambda g, sc0=sc0: g.tensor_scalar_mul(stats[:, sc0:sc0 + 1], stats[:, sc0:sc0 + 1], 1.0 / DM), t1, vself())
                tq2 = ev("vector", lambda g, sc0=sc0: g.tensor_scalar_mul(stats[:, sc0 + 1:sc0 + 2], stats[:, sc0 + 1:sc0 + 2], 1.0 / DM), t2, vself())
                tv = ev("vector", lambda g, sc0=sc0: g.tensor_tensor(stats[:, sc0 + 2:sc0 + 3], stats[:, sc0:sc0 + 1], stats[:, sc0:sc0 + 1], ALU.mult), tm, vself())
                tv2 = ev("vector", lambda g, sc0=sc0: g.tensor_sub(stats[:, sc0 + 1:sc0 + 2], stats[:, sc0 + 1:sc0 + 2], stats[:, sc0 + 2:sc0 + 3]), tq2, tv, vself())
                tsd = ev("scalar", lambda g, sc0=sc0: g.activation(stats[:, sc0 + 2:sc0 + 3], stats[:, sc0 + 1:sc0 + 2], ACT.Sqrt, bias=np.float32(1e-5)), tv2)
                tr = ev("vector", lambda g, sc0=sc0: g.reciprocal(stats[:, sc0 + 3:sc0 + 4], stats[:, sc0 + 2:sc0 + 3]), tsd)
                tmr = ev("vector", lambda g, sc0=sc0: g.scalar_tensor_tensor(stats[:, sc0 + 4:sc0 + 5], stats[:, sc0:sc0 + 1], -1.0, stats[:, sc0 + 3:sc0 + 4], ALU.mult, ALU.mult), tr, vself())
                txn = ev("scalar", lambda g, c=c, sc0=sc0: g.activation(
                    sqt[:], xb[c][:], ACT.Identity, bias=stats[:, sc0 + 4:sc0 + 5], scale=stats[:, sc0 + 3:sc0 + 4]),
                    merge(tr, tmr), tk_sq, tk_x[c], t2)
                tk_xfree[c % 2] = txn
                for grp in range(4):
                    tts = []
                    for k in range(4):
                        dt_ = grp * 4 + k
                        tts.append(ev("tensor", lambda g, dt_=dt_, k=k: g.transpose(
                            psT[:, k * 128:(k + 1) * 128], sqt[:, dt_ * 128:(dt_ + 1) * 128], identb[:]),
                            txn, tk_idb, state["tk_psT"]))
                    def _evx(g, grp=grp, c=c):
                        for k in range(3):
                            g.tensor_copy(xnf[:, (grp * 4 + k) * 512 + c * 128:(grp * 4 + k) * 512 + c * 128 + 128],
                                          psT[:, k * 128:(k + 1) * 128])
                        return g.tensor_copy(xnf[:, (grp * 4 + 3) * 512 + c * 128:(grp * 4 + 3) * 512 + c * 128 + 128],
                                             psT[:, 384:512])
                    tev = ev("vector", _evx, merge(*tts))
                    state["tk_psT"] = tev
                    tk_xnf.append(tev)
                tk_sq = tk_xnf[-1]
            tk_xnf_all = merge(*tk_xnf)

            pshalf = [(ps0, 0), (ps0, 512), (ps1, 0), (ps1, 512)]
            psfree = {0: state["tk_ps0"], 1: state["tk_ps0"], 2: state["tk_ps1"], 3: state["tk_ps1"]}

            tk_kf = {}
            tk_tmp = None
            for g_ in range(4):
                ps, off = pshalf[g_]
                mm = None
                for dt_ in range(16):
                    mm = ev("tensor", lambda g, ps=ps, off=off, dt_=dt_, g_=g_, s=(dt_ == 0), p=(dt_ == 15): g.matmul(
                        ps[:, off:off + 512], poolW[:, dt_ * 512 + g_ * 128:dt_ * 512 + g_ * 128 + 128],
                        xnf[:, dt_ * 512:(dt_ + 1) * 512], start=s, stop=p),
                        tk_wk[dt_], tk_xnf_all, psfree[g_])
                tk = rope_elu(ps, off, kf, g_ * 512, mm, tk_cos, tk_tmp)
                tk_tmp = tk
                tk_kf[g_] = tk
                psfree[g_] = tk
            tk_vt = {}
            for c in range(4):
                ps, off = pshalf[c]
                mm = None
                for dt_ in range(16):
                    mm = ev("tensor", lambda g, ps=ps, off=off, dt_=dt_, c=c, s=(dt_ == 0), p=(dt_ == 15): g.matmul(
                        ps[:, off:off + 512],
                        xnf[:, dt_ * 512 + c * 128:dt_ * 512 + c * 128 + 128],
                        poolW[:, 8192 + dt_ * 512:8192 + (dt_ + 1) * 512], start=s, stop=p),
                        tk_wv[dt_], tk_xnf_all, psfree[c])
                tk_vt[c] = ev("scalar", lambda g, ps=ps, off=off, c=c: g.activation(
                    vt[:, c * 512:(c + 1) * 512], ps[:, off:off + 512], ACT.Copy), mm)
                psfree[c] = tk_vt[c]
            tk_kft = {}
            for c in range(4):
                for g_ in range(4):
                    tt = ev("tensor", lambda g, c=c, g_=g_: g.transpose(
                        psT[:, g_ * 128:(g_ + 1) * 128],
                        kf[:, g_ * 512 + c * 128:g_ * 512 + c * 128 + 128], identb[:]),
                        tk_kf[g_], state["tk_psT"])
                    tev = ev("vector", lambda g, c=c, g_=g_: g.tensor_copy(
                        kft[:, (c * 4 + g_) * 128:(c * 4 + g_) * 128 + 128],
                        psT[:, g_ * 128:(g_ + 1) * 128]), tt)
                    state["tk_psT"] = tev
                    tk_kft[(c, g_)] = tev
            tk_qf = {}
            if not is_prefix:
                for h in range(16):
                    ps, off = pshalf[h % 4]
                    mm = None
                    for dt_ in range(16):
                        mm = ev("tensor", lambda g, ps=ps, off=off, dt_=dt_, h=h, s=(dt_ == 0), p=(dt_ == 15): g.matmul(
                            ps[:, off:off + 512], pool1[:, dt_ * 2048 + h * 128:dt_ * 2048 + h * 128 + 128],
                            xnf[:, dt_ * 512:(dt_ + 1) * 512], start=s, stop=p),
                            tk_wq[(dt_, h // 4)], tk_xnf_all, psfree[h % 4])
                    tk = rope_elu(ps, off, qf16, h * 512, mm, tk_cos, tk_tmp)
                    tk_tmp = tk
                    tk_qf[h] = tk
                    psfree[h % 4] = tk
            state["tk_ps0"] = merge(psfree[0], psfree[1])
            state["tk_ps1"] = merge(psfree[2], psfree[3])

            for c in range(4):
                def _zred(g, c=c):
                    for g_ in range(3):
                        g.reduce_sum(zr[:, g_:g_ + 1], kf[:, g_ * 512 + c * 128:g_ * 512 + c * 128 + 128], AX.X)
                    return g.reduce_sum(zr[:, 3:4], kf[:, 3 * 512 + c * 128:3 * 512 + c * 128 + 128], AX.X)
                tkz1 = ev("vector", _zred, merge(*tk_kf.values()), state["tk_zb"])
                tkz2 = ev("vector", lambda g: g.tensor_add(zf[:], zf[:], zr[:]), tkz1, state["tk_zf"], vself())
                state["tk_zf"] = tkz2
                stmm = []
                for g_ in range(4):
                    stmm.append(ev("tensor", lambda g, c=c, g_=g_: g.matmul(
                        ps2[:, g_ * 128:(g_ + 1) * 128],
                        kft[:, (c * 4 + g_) * 128:(c * 4 + g_) * 128 + 128],
                        vt[:, c * 512 + g_ * 128:c * 512 + g_ * 128 + 128],
                        start=True, stop=True),
                        tk_kft[(c, g_)], tk_vt[c], state["tk_ps2"]))
                tk_stmm = merge(*stmm)
                if is_prefix:
                    tka = ev("vector", lambda g: g.tensor_tensor(stf[:], stf[:], ps2[:], ALU.add),
                             tk_stmm, state["tk_stf"], vself())
                    state["tk_stf"] = tka
                    state["tk_ps2"] = tka
                    continue
                num_done = []
                den_done = []
                abf_tok = []
                for hb in range(2):
                    amms = []
                    for hh in range(8):
                        h = hb * 8 + hh
                        g_ = h // 4
                        amms.append(ev("tensor", lambda g, c=c, h=h, g_=g_, hh=hh: g.matmul(
                            ps0[:, hh * 128:(hh + 1) * 128],
                            kf[:, g_ * 512 + c * 128:g_ * 512 + c * 128 + 128],
                            qf16[:, h * 512 + c * 128:h * 512 + c * 128 + 128],
                            start=True, stop=True),
                            tk_kf[g_], tk_qf[h], state["tk_ps0"]))
                    tk_abf = ev("vector", lambda g, hb=hb: g.tensor_tensor(
                        abf[:].rearrange("p (h j) -> p h j", h=8),
                        ps0[:].rearrange("p (h j) -> p h j", h=8),
                        maskb[:].unsqueeze(1).broadcast_to([128, 8, 128]),
                        ALU.mult), merge(*amms), tk_mask,
                        merge(*(num_done + den_done)) if hb == 1 else state.get("tk_abf_last"))
                    state["tk_ps0"] = tk_abf
                    dm2 = []
                    nm2 = []
                    for hh in range(8):
                        h = hb * 8 + hh
                        g_ = h // 4
                        ev("tensor", lambda g, hh=hh, h=h: g.matmul(
                            psD[:, h:h + 1], abf[:, hh * 128:(hh + 1) * 128],
                            onesb[:], start=True, stop=False),
                            tk_abf, tk_ones, state["tk_psD"])
                        dm2.append(ev("tensor", lambda g, h=h, g_=g_, c=c: g.matmul(
                            psD[:, h:h + 1],
                            qf16[:, h * 512 + c * 128:h * 512 + c * 128 + 128],
                            zb[:, g_:g_ + 1], start=False, stop=True),
                            state["tk_zb"], tk_qf[h]))
                        ev("tensor", lambda g, hh=hh, h=h, g_=g_, c=c: g.matmul(
                            ps1[:, hh * 128:(hh + 1) * 128],
                            qf16[:, h * 512 + c * 128:h * 512 + c * 128 + 128],
                            stb[:, g_ * 128:(g_ + 1) * 128], start=True, stop=False),
                            state["tk_stb"], tk_qf[h], state["tk_ps1"])
                        nm2.append(ev("tensor", lambda g, hh=hh, g_=g_, c=c: g.matmul(
                            ps1[:, hh * 128:(hh + 1) * 128],
                            abf[:, hh * 128:(hh + 1) * 128],
                            vt[:, c * 512 + g_ * 128:c * 512 + g_ * 128 + 128],
                            start=False, stop=True),
                            tk_abf, tk_vt[c]))
                    den_done = dm2
                    num_done = nm2
                    state["tk_abf_last"] = merge(*(dm2 + nm2))
                    tkd1 = ev("vector", lambda g, hb=hb: g.tensor_scalar_add(
                        dent[:, hb * 8:(hb + 1) * 8], psD[:, hb * 8:(hb + 1) * 8], 1e-6), merge(*dm2))
                    tkd2 = ev("vector", lambda g, hb=hb: g.reciprocal(
                        rden[:, hb * 8:(hb + 1) * 8], dent[:, hb * 8:(hb + 1) * 8]), tkd1, vself())
                    state["tk_psD"] = tkd1
                    tko = ev("vector", lambda g, hb=hb: g.tensor_tensor(
                        otm[:, hb * 1024:(hb + 1) * 1024].rearrange("p (h d) -> p h d", h=8),
                        ps1[:].rearrange("p (h d) -> p h d", h=8),
                        rden[:, hb * 8:(hb + 1) * 8].unsqueeze(2).broadcast_to([128, 8, 128]),
                        ALU.mult), tkd2, merge(*nm2), state.get("tk_otm_rd"), vself())
                    state["tk_ps1"] = tko
                    if hb == 0:
                        tk_otm0 = tko
                    else:
                        tk_otm1 = tko
                tka = ev("vector", lambda g: g.tensor_tensor(stf[:], stf[:], ps2[:], ALU.add),
                         tk_stmm, state["tk_stf"], merge(*num_done), vself())
                state["tk_stf"] = tka
                state["tk_ps2"] = tka
                state["tk_stb"] = ev("scalar", lambda g: g.activation(stb[:], stf[:], ACT.Copy), tka)
                state["tk_zb"] = ev("scalar", lambda g: g.activation(zb[:], zf[:], ACT.Copy),
                                    state["tk_zf"], merge(*den_done))
                tk_otm = merge(tk_otm0, tk_otm1)
                wr = []
                for grp in range(4):
                    tts = []
                    for k in range(4):
                        h = grp * 4 + k
                        tts.append(ev("tensor", lambda g, h=h, k=k: g.transpose(
                            psT[:, k * 128:(k + 1) * 128], otm[:, h * 128:(h + 1) * 128], identb[:]),
                            tk_otm, state["tk_psT"]))
                    tev = ev("vector", lambda g: g.tensor_copy(otf[:], psT[:]),
                             merge(*tts), state.get("tk_otf_wr"))
                    state["tk_psT"] = tev
                    tcol = (sc * NCH + c) * 128
                    tkw = ev("gpsimd", lambda g, grp=grp, tcol=tcol: g.dma_start(
                        out=s_out[grp * 512:(grp + 1) * 512, tcol:tcol + 128].rearrange("(k p) t -> p k t", p=128),
                        in_=otf[:].rearrange("p (k t) -> p k t", k=4)), tev)
                    state["tk_otf_wr"] = tkw
                    wr.append(tkw)
                state["tk_otm_rd"] = merge(*tts)

        def dump_sc0():
            ev("gpsimd", lambda g: g.dma_start(out=dbg_kf, in_=kf[:]), dict(cnt))
            ev("gpsimd", lambda g: g.dma_start(out=dbg_vt, in_=vt[:]))
            ev("gpsimd", lambda g: g.dma_start(out=dbg_qf, in_=qf16[:]))
            ev("gpsimd", lambda g: g.dma_start(out=dbg_xnf, in_=xnf[:]))

        super_chunk(0, True)
        for dt_ in range(16):
            for p in range(4):
                tk_wq[(dt_, p)] = wload(si, pool1, dt_ * 2048 + p * 512,
                                        wqT[dt_ * 128:(dt_ + 1) * 128, p * 512:(p + 1) * 512]); si += 1
        for sc in range(1, NSC):
            super_chunk(sc, True)
        bar()
        tkg1 = ev("vector", lambda g: g.tensor_single_scalar(stf[:], stf[:], pfl[:], ALU.mult), tk_pfl)
        state["tk_stf"] = tkg1
        state["tk_stb"] = ev("scalar", lambda g: g.activation(stb[:], stf[:], ACT.Copy), tkg1)
        tkg2 = ev("vector", lambda g: g.tensor_single_scalar(zf[:], zf[:], pfl[:], ALU.mult), tk_pfl, vself())
        state["tk_zf"] = tkg2
        state["tk_zb"] = ev("scalar", lambda g: g.activation(zb[:], zf[:], ACT.Copy), tkg2)
        if DEBUG:
            ev("gpsimd", lambda g: g.dma_start(out=dbg_st0, in_=stb[:]), state["tk_stb"])
            tkzd = ev("gpsimd", lambda g: g.dma_start(out=dbg_zb0, in_=zb[:]), state["tk_zb"])
            state["tk_stb"] = merge(state["tk_stb"], tkzd)
        for sc in range(NSC):
            super_chunk(sc, False)
            if DEBUG and sc == 0:
                dump_sc0()
        bar()

        tk_wo = {}
        for dt_ in range(16):
            for p in range(4):
                tk_wo[(dt_, p)] = wload(si, pool1, dt_ * 2048 + p * 512,
                                        woT[dt_ * 128:(dt_ + 1) * 128, p * 512:(p + 1) * 512]); si += 1
        tk_kft_rd = None
        tk_x_rd = [None, None]
        tk_Tc_rd = merge(*tk_wo.values())
        for tt in range(16):
            d1 = ev("gpsimd", lambda g, tt=tt: g.dma_start(
                out=kft[:, 0:1024].rearrange("p (k t) -> p k t", k=8),
                in_=s_out[0:1024, tt * 128:(tt + 1) * 128].rearrange("(k p) t -> p k t", p=128)), tk_kft_rd)
            d2 = ev("gpsimd", lambda g, tt=tt: g.dma_start(
                out=kft[:, 1024:2048].rearrange("p (k t) -> p k t", k=8),
                in_=s_out[1024:2048, tt * 128:(tt + 1) * 128].rearrange("(k p) t -> p k t", p=128)), tk_kft_rd)
            tk_of = merge(d1, d2)
            xb_ = T_a if tt % 2 == 0 else T_b
            tk_xd = ev("sync", lambda g, tt=tt, xb_=xb_: g.dma_start(
                out=xb_[:], in_=x_own[tt * 128:(tt + 1) * 128, :]), tk_x_rd[tt % 2])
            evs = []
            mmlast = None
            for dmc in range(4):
                ps, off = [(ps0, 0), (ps0, 512), (ps1, 0), (ps1, 512)][dmc]
                pfree = state["tk_ps0"] if dmc < 2 else state["tk_ps1"]
                for ft in range(16):
                    mmlast = ev("tensor", lambda g, ps=ps, off=off, ft=ft, dmc=dmc, s=(ft == 0), p=(ft == 15): g.matmul(
                        ps[:, off:off + 512], kft[:, ft * 128:(ft + 1) * 128],
                        pool1[:, ft * 2048 + dmc * 512:ft * 2048 + dmc * 512 + 512], start=s, stop=p),
                        tk_of, tk_wo[(ft, dmc)], pfree)
                tke = ev("vector", lambda g, ps=ps, off=off, dmc=dmc, xb_=xb_: g.tensor_tensor(
                    T_c[:, dmc * 512:(dmc + 1) * 512], ps[:, off:off + 512],
                    xb_[:, dmc * 512:(dmc + 1) * 512], ALU.add), mmlast, tk_xd, tk_Tc_rd)
                evs.append(tke)
                if dmc < 2:
                    state["tk_ps0"] = tke
                else:
                    state["tk_ps1"] = tke
            tk_kft_rd = mmlast
            tk_x_rd[tt % 2] = merge(*evs)
            tk_Tc_rd = ev("gpsimd", lambda g, tt=tt: g.dma_start(
                out=s_x1[tt * 128:(tt + 1) * 128, :], in_=T_c[:]), merge(*evs))
        bar()

        ps_tiles = []
        for i in range(16):
            ps_tiles.append((wupT[i * 128:(i + 1) * 128, 0:2048], 0))
            ps_tiles.append((wupT[i * 128:(i + 1) * 128, 2048:4096], 0))
        for i in range(32):
            ps_tiles.append((wdnT[i * 128:(i + 1) * 128, :], 1))
        C6 = float(1.5 * 2 ** 23 * 2 ** -6)
        slot_rd = [None, None]
        slot_dma = [None, None]
        tk_accw = None
        sbuf2 = [T_a, T_b]
        tmps = [tmp1, tmp2]
        for i, (ap, col) in enumerate(ps_tiles):
            sl = i % 2
            tkd = ev("sync", lambda g, ap=ap, sl=sl: g.dma_start(out=sbuf2[sl][:], in_=ap), slot_rd[sl])
            tkr = ev("vector", lambda g, sl=sl: g.reduce_sum(tmps[sl][:, 0:1], sbuf2[sl][:], AX.X, apply_absolute_value=True), tkd, slot_dma[sl])
            slot_rd[sl] = tkr
            t_hi = ev("vector", lambda g, sl=sl: g.tensor_scalar(tmps[sl][:, 1:2], tmps[sl][:, 0:1], C6, C6, ALU.add, ALU.subtract), tkr, vself())
            t_lo = ev("vector", lambda g, sl=sl: g.tensor_sub(tmps[sl][:, 0:1], tmps[sl][:, 0:1], tmps[sl][:, 1:2]), t_hi, vself())
            tka = ev("vector", lambda g, sl=sl, col=col: g.tensor_tensor(
                acc[:, col:col + 1], acc[:, col:col + 1], tmps[sl][:, 1:2], ALU.add), t_hi, tk_accw, vself())
            tka2 = ev("vector", lambda g, sl=sl, col=col: g.tensor_tensor(
                acc[:, 2 + col:3 + col], acc[:, 2 + col:3 + col], tmps[sl][:, 0:1], ALU.add), t_lo, tka, vself())
            slot_dma[sl] = tka2
            tk_accw = tka2
        tko3 = ev("vector", lambda g: g.memset(T_c[:, 0:128], 1.0), merge(*[t for t in [slot_dma[0], slot_dma[1]] if t]))
        tkmm = ev("tensor", lambda g: g.matmul(psD[:, 0:4], T_c[:, 0:128], acc[:],
                                               start=True, stop=True), tko3, tk_accw, state["tk_psD"])
        tkc0 = ev("vector", lambda g: g.tensor_copy(stats[:, 0:4], psD[:, 0:4]), tkmm)
        tkf0 = ev("vector", lambda g: g.tensor_tensor(scl[:], stats[:, 0:2], stats[:, 2:4], ALU.add), tkc0, vself())
        tkf = ev("vector", lambda g: g.tensor_scalar(scl[:], scl[:], float(2.0 ** -23), 1e-5, ALU.mult, ALU.max), tkf0, vself())
        state["tk_psD"] = tkf
        tkh = ev("vector", lambda g: g.tensor_scalar_mul(hsc[:], scl[:], 0.5), tkf, vself())
        tkn = ev("vector", lambda g: g.tensor_scalar_mul(nhs[:], scl[:], -0.5), tkf, vself())
        bar()

        tk_aq1 = {}
        prevload = None
        prev_done = None
        for tt in range(16):
            tkd = ev("sync", lambda g, tt=tt: g.dma_start(out=T_a[:], in_=s_x1[tt * 128:(tt + 1) * 128, :]), prevload)
            t1 = ev("vector", lambda g: g.reduce_sum(stats[:, 0:1], T_a[:], AX.X), tkd, prev_done)
            tq = ev("scalar", lambda g: g.activation(T_b[:], T_a[:], ACT.Square), tkd, prev_done)
            t2 = ev("vector", lambda g: g.reduce_sum(stats[:, 1:2], T_b[:], AX.X), tq)
            tm = ev("vector", lambda g: g.tensor_scalar_mul(stats[:, 0:1], stats[:, 0:1], 1.0 / DM), t1, vself())
            tqq = ev("vector", lambda g: g.tensor_scalar_mul(stats[:, 1:2], stats[:, 1:2], 1.0 / DM), t2, vself())
            tv = ev("vector", lambda g: g.tensor_tensor(stats[:, 2:3], stats[:, 0:1], stats[:, 0:1], ALU.mult), tm, vself())
            tv2 = ev("vector", lambda g: g.tensor_sub(stats[:, 1:2], stats[:, 1:2], stats[:, 2:3]), tqq, tv, vself())
            tsd = ev("scalar", lambda g: g.activation(stats[:, 2:3], stats[:, 1:2], ACT.Sqrt, bias=np.float32(1e-5)), tv2)
            tr = ev("vector", lambda g: g.reciprocal(stats[:, 3:4], stats[:, 2:3]), tsd)
            tt1 = ev("vector", lambda g: g.tensor_single_scalar(T_c[:], T_a[:], stats[:, 0:1], ALU.subtract), tm, tkd)
            tam = ev("vector", lambda g: g.reduce_max(stats[:, 4:5], T_c[:], AX.X, apply_absolute_value=True), tt1, vself())
            tl = ev("vector", lambda g: g.tensor_tensor(stats[:, 5:6], stats[:, 4:5], stats[:, 3:4], ALU.mult), tam, tr, vself())
            tl = ev("vector", lambda g: g.tensor_scalar_max(stats[:, 5:6], stats[:, 5:6], 1e-5), tl, vself())
            tl2 = ev("vector", lambda g: g.reciprocal(stats[:, 6:7], stats[:, 5:6]), tl, vself())
            tl3 = ev("vector", lambda g: g.tensor_tensor(stats[:, 7:8], stats[:, 3:4], stats[:, 6:7], ALU.mult), tl2, vself())
            tl3 = ev("vector", lambda g: g.tensor_scalar_mul(stats[:, 7:8], stats[:, 7:8], 127.0), tl3, vself())
            tc1 = ev("vector", lambda g, tt=tt: g.tensor_tensor(cfac[:, tt:tt + 1], scl[:, 0:1], stats[:, 5:6], ALU.mult), tl, vself())
            tc1 = ev("vector", lambda g, tt=tt: g.tensor_scalar_mul(cfac[:, tt:tt + 1], cfac[:, tt:tt + 1], 1.0 / 127.0), tc1, vself())
            ty = ev("vector", lambda g: g.tensor_single_scalar(T_c[:], T_c[:], stats[:, 7:8], ALU.mult), tl3, tam, vself())
            ty = ev("vector", lambda g: g.tensor_scalar(T_c[:], T_c[:], CLIP, -CLIP, ALU.min, ALU.max), ty, vself())
            ta_ = ev("vector", lambda g: g.tensor_scalar(T_b[:], T_c[:], C23, C23, ALU.add, ALU.subtract), ty, vself())
            tc_ = ev("vector", lambda g: g.tensor_scalar(T_c[:], T_b[:], C52, C52, ALU.add, ALU.subtract), ta_, vself())
            ts_ = ev("vector", lambda g: g.tensor_sub(T_c[:], T_c[:], T_b[:]), tc_, vself())
            taq = ev("vector", lambda g: g.scalar_tensor_tensor(sqt[:], T_c[:], fdom[:], T_b[:], ALU.mult, ALU.add), ts_, tk_fdom, t2, vself())
            trans = []
            for grp in range(4):
                tts = []
                for k in range(4):
                    dt_ = grp * 4 + k
                    tts.append(ev("tensor", lambda g, dt_=dt_, k=k: g.transpose(
                        psT[:, k * 128:(k + 1) * 128], sqt[:, dt_ * 128:(dt_ + 1) * 128], identb[:]),
                        taq, state["tk_psT"]))
                def _eva(g, grp=grp, tt=tt):
                    for k in range(3):
                        g.tensor_copy(pool1[:, (grp * 4 + k) * 2048 + tt * 128:(grp * 4 + k) * 2048 + tt * 128 + 128],
                                      psT[:, k * 128:(k + 1) * 128])
                    return g.tensor_copy(pool1[:, (grp * 4 + 3) * 2048 + tt * 128:(grp * 4 + 3) * 2048 + tt * 128 + 128],
                                         psT[:, 384:512])
                tev = ev("vector", _eva, merge(*tts))
                state["tk_psT"] = tev
                trans.append(tev)
            tk_aq1[tt] = merge(*trans)
            prevload = merge(tt1, tq, t1)
            prev_done = merge(*trans)
        bar()

        tk_ru_wr = []
        tk_wup_rd = None
        tk_rum = {t: None for t in range(16)}
        slotC_rd = [None, None, None, None]
        slotA_rd = [None, None, None, None]
        for oc in range(8):
            tkq_tiles = []
            for dt_ in range(16):
                a = (dt_ % 4) * 512
                tkd = ev("sync", lambda g, dt_=dt_, oc=oc, a=a: g.dma_start(
                    out=T_a[:, a:a + 512], in_=wupT[dt_ * 128:(dt_ + 1) * 128, oc * 512:(oc + 1) * 512]),
                    slotA_rd[dt_ % 4])
                tkl = ev("vector", lambda g, a=a: g.tensor_single_scalar(T_b[:, a:a + 512], T_a[:, a:a + 512], nhs[:, 0:1], ALU.is_lt), tkd)
                tkq = ev("vector", lambda g, dt_=dt_, a=a: g.scalar_tensor_tensor(
                    qf16[:, dt_ * 512:(dt_ + 1) * 512], T_a[:, a:a + 512], hsc[:, 0:1], T_b[:, a:a + 512],
                    ALU.is_gt, ALU.subtract), tkl, tk_wup_rd, vself())
                tkq_tiles.append(tkq)
                slotA_rd[dt_ % 4] = tkq
            tk_wupq = merge(*tkq_tiles)
            mm_all = []
            for tt in range(16):
                ps, off = [(ps0, 0), (ps0, 512), (ps1, 0), (ps1, 512)][tt % 4]
                pfree = state["tk_ps0"] if tt % 4 < 2 else state["tk_ps1"]
                mm = None
                for dt_ in range(16):
                    mm = ev("tensor", lambda g, ps=ps, off=off, dt_=dt_, tt=tt, s=(dt_ == 0), p=(dt_ == 15): g.matmul(
                        ps[:, off:off + 512],
                        pool1[:, dt_ * 2048 + tt * 128:dt_ * 2048 + tt * 128 + 128],
                        qf16[:, dt_ * 512:(dt_ + 1) * 512], start=s, stop=p),
                        tk_aq1[tt], tk_wupq, pfree)
                mm_all.append(mm)
                slot = tt % 4
                tkru = ev("vector", lambda g, ps=ps, off=off, slot=slot: g.tensor_scalar_max(
                    T_c[:, slot * 512:(slot + 1) * 512], ps[:, off:off + 512], 0.0), mm, slotC_rd[slot])
                if tt % 4 < 2:
                    state["tk_ps0"] = tkru
                else:
                    state["tk_ps1"] = tkru
                tkrm = ev("vector", lambda g, slot=slot: g.reduce_max(stats[:, 8 + slot:9 + slot], T_c[:, slot * 512:(slot + 1) * 512], AX.X), tkru, vself())
                tkac = ev("vector", lambda g, tt=tt, slot=slot: g.tensor_tensor(
                    rumac[:, tt:tt + 1], rumac[:, tt:tt + 1], stats[:, 8 + slot:9 + slot], ALU.max), tkrm, tk_rum[tt], vself())
                tk_rum[tt] = tkac
                tkw = ev("gpsimd", lambda g, tt=tt, oc=oc, slot=slot: g.dma_start(
                    out=s_ru[tt * 128:(tt + 1) * 128, oc * 512:(oc + 1) * 512],
                    in_=T_c[:, slot * 512:(slot + 1) * 512]), tkru)
                slotC_rd[slot] = tkw
                tk_ru_wr.append(tkw)
            tk_wup_rd = merge(*mm_all)
            bar()
        bar()

        tk_a_rd = None
        tk_b_rd = None
        tk_sq_rd = None
        for i in range(32):
            tkd = ev("sync", lambda g, i=i: g.dma_start(out=T_a[:], in_=wdnT[i * 128:(i + 1) * 128, :]), tk_a_rd)
            tkl = ev("vector", lambda g: g.tensor_single_scalar(T_b[:], T_a[:], nhs[:, 1:2], ALU.is_lt), tkd, tk_b_rd)
            tkq = ev("vector", lambda g: g.scalar_tensor_tensor(sqt[:], T_a[:], hsc[:, 1:2], T_b[:], ALU.is_gt, ALU.subtract), tkl, tk_sq_rd, vself())
            tk_a_rd = tkq
            tk_b_rd = tkq
            tkw = ev("gpsimd", lambda g, i=i: g.dma_start(out=s_wdn[i * 128:(i + 1) * 128, :], in_=sqt[:]), tkq)
            tk_sq_rd = tkw
        bar()

        for half in range(2):
            tk_aq2 = {}
            prevl = None
            prevd = None
            for ti in range(8):
                tt = half * 8 + ti
                d1 = ev("sync", lambda g, tt=tt: g.dma_start(out=T_a[:], in_=s_ru[tt * 128:(tt + 1) * 128, 0:2048]), prevl)
                d2 = ev("sync", lambda g, tt=tt: g.dma_start(out=T_b[:], in_=s_ru[tt * 128:(tt + 1) * 128, 2048:4096]), prevl)
                t1 = ev("vector", lambda g, tt=tt: g.tensor_tensor(stats[:, 0:1], rumac[:, tt:tt + 1], cfac[:, tt:tt + 1], ALU.mult), prevd)
                t2 = ev("vector", lambda g: g.tensor_tensor(stats[:, 1:2], stats[:, 0:1], stats[:, 0:1], ALU.mult), t1, vself())
                t3 = ev("vector", lambda g: g.tensor_scalar_max(stats[:, 1:2], stats[:, 1:2], 1e-5), t2, vself())
                t4 = ev("vector", lambda g, tt=tt: g.tensor_tensor(stats[:, 2:3], cfac[:, tt:tt + 1], cfac[:, tt:tt + 1], ALU.mult), prevd)
                t5 = ev("vector", lambda g: g.reciprocal(stats[:, 3:4], stats[:, 1:2]), t3)
                t6 = ev("vector", lambda g: g.tensor_tensor(stats[:, 2:3], stats[:, 2:3], stats[:, 3:4], ALU.mult), t4, t5, vself())
                t7 = ev("vector", lambda g: g.tensor_scalar_mul(stats[:, 2:3], stats[:, 2:3], 127.0), t6, vself())
                t8 = ev("scalar", lambda g, tt=tt: g.activation(sq2s[:, tt:tt + 1], stats[:, 2:3], ACT.Sqrt), t7)
                t9 = ev("vector", lambda g, tt=tt: g.tensor_tensor(resc[:, tt:tt + 1], scl[:, 1:2], stats[:, 1:2], ALU.mult), t3, vself())
                t9 = ev("vector", lambda g, tt=tt: g.tensor_scalar_mul(resc[:, tt:tt + 1], resc[:, tt:tt + 1], 1.0 / 127.0), t9, vself())
                aq_done = []
                for hp, buf, dtk in ((0, T_a, d1), (1, T_b, d2)):
                    u1 = ev("vector", lambda g, buf=buf, tt=tt: g.tensor_single_scalar(buf[:], buf[:], sq2s[:, tt:tt + 1], ALU.mult), t8, dtk)
                    u2 = ev("scalar", lambda g, buf=buf: g.activation(buf[:], buf[:], ACT.Square), u1)
                    u3 = ev("vector", lambda g, buf=buf: g.tensor_scalar(T_c[:], buf[:], C23, C23, ALU.add, ALU.subtract), u2, slotC_rd[0])
                    u4 = ev("vector", lambda g, buf=buf: g.tensor_scalar(buf[:], T_c[:], C52, C52, ALU.add, ALU.subtract), u3, vself())
                    u5 = ev("vector", lambda g, buf=buf: g.tensor_sub(buf[:], buf[:], T_c[:]), u4, vself())
                    u6 = ev("vector", lambda g, buf=buf, hp=hp: g.scalar_tensor_tensor(
                        xnf[:, hp * 2048:(hp + 1) * 2048], buf[:], fdom[:], T_c[:], ALU.mult, ALU.add), u5, vself())
                    slotC_rd[0] = u6
                    aq_done.append(u6)
                tk_aq2tm = merge(*aq_done)
                trans = []
                for grp in range(8):
                    tts = []
                    for k in range(4):
                        ot = grp * 4 + k
                        tts.append(ev("tensor", lambda g, ot=ot, k=k: g.transpose(
                            psT[:, k * 128:(k + 1) * 128], xnf[:, ot * 128:(ot + 1) * 128], identb[:]),
                            tk_aq2tm, state["tk_psT"]))
                    def _evb(g, grp=grp, ti=ti):
                        for k in range(3):
                            g.tensor_copy(pool1[:, (grp * 4 + k) * 1024 + ti * 128:(grp * 4 + k) * 1024 + ti * 128 + 128],
                                          psT[:, k * 128:(k + 1) * 128])
                        return g.tensor_copy(pool1[:, (grp * 4 + 3) * 1024 + ti * 128:(grp * 4 + 3) * 1024 + ti * 128 + 128],
                                             psT[:, 384:512])
                    tev = ev("vector", _evb, merge(*tts))
                    state["tk_psT"] = tev
                    trans.append(tev)
                tk_aq2[ti] = merge(*trans)
                prevl = merge(aq_done[0], aq_done[1])
                prevd = merge(*trans)
            bar()
            tk_wdn_rd = None
            tk_x1_rd = [None, None]
            tk_y_rd = [None, None]
            for dmc in range(4):
                wdl = []
                for g4 in range(4):
                    wdl.append(ev("sync", lambda g, g4=g4, dmc=dmc: g.dma_start(
                        out=poolW[:, g4 * 4096:(g4 + 1) * 4096].rearrange("p (o d) -> p o d", o=8),
                        in_=s_wdn[g4 * 1024:(g4 + 1) * 1024, dmc * 512:(dmc + 1) * 512].rearrange("(o p) d -> p o d", p=128)),
                        tk_wdn_rd))
                tk_wd = merge(*wdl)
                mm_all = []
                for ti in range(8):
                    tt = half * 8 + ti
                    ps, off = [(ps0, 0), (ps0, 512)][ti % 2]
                    pfree = state["tk_ps0"]
                    xsl = ti % 2
                    tkx = ev("sync", lambda g, tt=tt, dmc=dmc, xsl=xsl: g.dma_start(
                        out=T_a[:, xsl * 512:(xsl + 1) * 512] if False else T_a[:, xsl * 512:(xsl + 1) * 512],
                        in_=s_x1[tt * 128:(tt + 1) * 128, dmc * 512:(dmc + 1) * 512]), tk_x1_rd[xsl])
                    mm = None
                    for ot in range(32):
                        mm = ev("tensor", lambda g, ps=ps, off=off, ot=ot, ti=ti, s=(ot == 0), p=(ot == 31): g.matmul(
                            ps[:, off:off + 512],
                            pool1[:, ot * 1024 + ti * 128:ot * 1024 + ti * 128 + 128],
                            poolW[:, ot * 512:(ot + 1) * 512], start=s, stop=p),
                            tk_aq2[ti], tk_wd, pfree)
                    mm_all.append(mm)
                    tky = ev("vector", lambda g, ps=ps, off=off, tt=tt, xsl=xsl: g.scalar_tensor_tensor(
                        T_b[:, xsl * 512:(xsl + 1) * 512], ps[:, off:off + 512], resc[:, tt:tt + 1],
                        T_a[:, xsl * 512:(xsl + 1) * 512], ALU.mult, ALU.add), mm, tkx, tk_y_rd[xsl])
                    state["tk_ps0"] = tky
                    tk_x1_rd[xsl] = tky
                    tkw = ev("gpsimd", lambda g, tt=tt, dmc=dmc, xsl=xsl: g.dma_start(
                        out=y_out[tt * 128:(tt + 1) * 128, dmc * 512:(dmc + 1) * 512],
                        in_=T_b[:, xsl * 512:(xsl + 1) * 512]), tky)
                    tk_y_rd[xsl] = tkw
                tk_wdn_rd = merge(*mm_all)
                bar()
            bar()

        if DEBUG:
            ev("gpsimd", lambda g: g.dma_start(out=dbg_out, in_=s_out))
            ev("gpsimd", lambda g: g.dma_start(out=dbg_x1, in_=s_x1))
            for _i in range(4):
                ev("gpsimd", lambda g, _i=_i: g.dma_start(out=dbg_ru[_i * 128:(_i + 1) * 128, :], in_=s_ru[_i * 128:(_i + 1) * 128, :]))
            for _i in range(8):
                ev("gpsimd", lambda g, _i=_i: g.dma_start(out=dbg_wdnq[_i * 512:(_i + 1) * 512, :], in_=s_wdn[_i * 512:(_i + 1) * 512, :]))
            ev("gpsimd", lambda g: g.dma_start(out=dbg_sc[:, 0:16], in_=rumac[:]))
            ev("gpsimd", lambda g: g.dma_start(out=dbg_sc[:, 16:32], in_=cfac[:]))
            ev("gpsimd", lambda g: g.dma_start(out=dbg_sc[:, 32:48], in_=resc[:]))
            ev("gpsimd", lambda g: g.dma_start(out=dbg_sc[:, 48:50], in_=scl[:]))
        final = dict(cnt)

        def runner(e):
            def _(eng):
                lw = {}
                for kind, w, fn in EV[e]:
                    for s_, v in sorted(w.items()):
                        if lw.get(s_, -1) < v:
                            eng.wait_ge(SEM[s_], v)
                            lw[s_] = v
                    if kind == "op":
                        fn(eng).then_inc(SEM[e], INC[e])
                for s_ in ENGINES:
                    if s_ != e and lw.get(s_, -1) < final[s_]:
                        eng.wait_ge(SEM[s_], final[s_])
            return _

        block.sync(runner("sync"))
        block.gpsimd(runner("gpsimd"))
        block.vector(runner("vector"))
        block.scalar(runner("scalar"))
        block.tensor(runner("tensor"))
    return nc


_cos_sin = None


def _tables():
    global _cos_sin
    if _cos_sin is None:
        inv = 1.0 / (10000.0 ** (np.arange(0, HD, 2, dtype=np.float64) / HD))
        t = np.arange(S, dtype=np.float64)
        fr = np.outer(t, inv)
        emb = np.concatenate([fr, fr], -1)
        _cos_sin = (np.cos(emb).T.astype(np.float32).copy(),
                    np.sin(emb).T.astype(np.float32).copy())
    return _cos_sin


def kernel(L0_attn_x, L0_attn_Wq, L0_attn_Wk, L0_attn_Wv, L0_attn_Wo,
           L0_ffn_w_up, L0_ffn_w_down):
    X = np.asarray(L0_attn_x, np.float32)
    wqT = np.ascontiguousarray(np.asarray(L0_attn_Wq, np.float32).T)
    wkT = np.ascontiguousarray(np.asarray(L0_attn_Wk, np.float32).T)
    wvT = np.ascontiguousarray(np.asarray(L0_attn_Wv, np.float32).T)
    woT = np.ascontiguousarray(np.asarray(L0_attn_Wo, np.float32).T)
    wupT = np.ascontiguousarray(np.asarray(L0_ffn_w_up, np.float32).T)
    wdnT = np.ascontiguousarray(np.asarray(L0_ffn_w_down, np.float32).T)
    cosF, sinF = _tables()
    maskT = np.triu(np.ones((128, 128), np.float32))
    identf = np.eye(128, dtype=np.float32)
    zeros = np.zeros((T, DM), np.float32)
    nc = build_program()
    in_maps = []
    for core in range(8):
        b, half = core // 2, core % 2
        off = half * T
        in_maps.append({
            "x_own": np.ascontiguousarray(X[b, off:off + T]),
            "x_pre": np.ascontiguousarray(X[b, 0:T]) if half else zeros,
            "pflag": np.full((128, 1), float(half), np.float32),
            "wqT": wqT, "wkT": wkT, "wvT": wvT, "woT": woT,
            "wupT": wupT, "wdnT": wdnT,
            "cos_own": np.ascontiguousarray(cosF[:, off:off + T]),
            "sin_own": np.ascontiguousarray(sinF[:, off:off + T]),
            "cos_pre": np.ascontiguousarray(cosF[:, 0:T]),
            "sin_pre": np.ascontiguousarray(sinF[:, 0:T]),
            "maskT": maskT, "identf": identf,
        })
    res = run_bass_kernel_spmd(nc, in_maps, core_ids=list(range(8)))
    out = np.empty((B, S, DM), np.float32)
    for core in range(8):
        b, half = core // 2, core % 2
        out[b, half * T:(half + 1) * T] = res.results[core]["y"]
    return out
